# version 03
# baseline (speedup 1.0000x reference)
import math
import os
import traceback
from contextlib import ExitStack

DEBUG_TB = bool(os.environ.get("K_DEBUG_TB"))

import numpy as np
import ml_dtypes

import concourse.bass as bass
import concourse.mybir as mybir
from concourse.bass_utils import run_bass_kernel_spmd

F32 = mybir.dt.float32
BF16 = mybir.dt.bfloat16
AF = mybir.ActivationFunctionType
ALU = mybir.AluOpType
AX = mybir.AxisListType

ENGINES = ("tensor", "vector", "scalar", "gpsimd", "sync")
DMA_QUEUES = ("sync", "scalar", "gpsimd")
NSEM_PER_QUEUE = 8


class Rsrc:
    __slots__ = ("name", "wd", "readers", "gen")

    def __init__(self, name="r"):
        self.name = name
        self.wd = {}
        self.readers = {}
        self.gen = 0


class Tile:
    __slots__ = ("t", "r")

    def __init__(self, t, name):
        self.t = t
        self.r = Rsrc(name)

    @property
    def ap(self):
        return self.t.ap() if hasattr(self.t, "ap") else self.t

    def __getitem__(self, idx):
        return self.t[idx]


class Dram:
    __slots__ = ("ap", "r")

    def __init__(self, ap, name):
        self.ap = ap
        self.r = Rsrc(name)

    def __getitem__(self, idx):
        return self.ap[idx]


class Prog:
    def __init__(self, nc):
        self.nc = nc
        self.ops = {e: [] for e in ENGINES}
        self.sem_names = []
        self.cnt = {}
        self.snaps = {}
        self.know = {e: {} for e in ENGINES}
        self.gen = 0
        for e in ENGINES:
            self._new_sem("S" + e)
        self.qsem = {}
        self.qidx = {}
        for q in DMA_QUEUES:
            self.qsem[q] = [self._new_sem(f"D{q}{i}") for i in range(NSEM_PER_QUEUE)]
            self.qidx[q] = 0
        self.const_arrays = {}
        self.const_tiles = {}
        self.need_consts = set()
        self._new_sem("Sconst")
        self._new_sem("Dcoll")
        self._es = ExitStack()
        self._scopes = [self._es]
        self._uid = 0
        self.NCONST = 48
        self._cp32 = self._es.enter_context(
            self.nc.sbuf_tensor("cpool32", [128, self.NCONST], F32))
        self._cp16 = self._es.enter_context(
            self.nc.sbuf_tensor("cpool16", [128, self.NCONST], BF16))

    def uid(self):
        self._uid += 1
        return self._uid

    def _new_sem(self, name):
        self.sem_names.append(name)
        self.cnt[name] = 0
        return name

    def sbuf(self, name, shape, dtype):
        t = self._scopes[-1].enter_context(
            self.nc.sbuf_tensor(f"{name}_{self.uid()}", shape, dtype))
        return Tile(t, name)

    def psum(self, name, shape, dtype=F32):
        t = self._scopes[-1].enter_context(
            self.nc.psum_tensor(f"{name}_{self.uid()}", shape, dtype))
        return Tile(t, name)

    def open_scope(self):
        self._scopes.append(ExitStack())

    def close_scope(self):
        self.barrier()
        self._scopes.pop().close()

    def dram(self, name, shape, dtype, kind="Internal"):
        if kind == "Internal":
            name = f"{name}_{self.uid()}"
        ap = self.nc.dram_tensor(name, shape, dtype, kind=kind).ap()
        return Dram(ap, name)

    def dram_in(self, name, shape, dtype):
        if not hasattr(self, "input_names"):
            self.input_names = []
        self.input_names.append(name)
        return self.dram(name, shape, dtype, kind="ExternalInput")

    def dram_out(self, name, shape, dtype):
        return self.dram(name, shape, dtype, kind="ExternalOutput")

    def _fresh(self, r):
        if r.gen != self.gen:
            r.gen = self.gen
            r.wd = {}
            r.readers = {}
        return r

    def _waits_for(self, eng, reads, writes, wappend=()):
        waits = {}
        for r in reads:
            for s, v in self._fresh(r).wd.items():
                if waits.get(s, 0) < v:
                    waits[s] = v
        for w in writes:
            self._fresh(w)
            for s, v in w.wd.items():
                if waits.get(s, 0) < v:
                    waits[s] = v
            for s, v in w.readers.items():
                if waits.get(s, 0) < v:
                    waits[s] = v
        for w in wappend:
            self._fresh(w)
        know = self.know[eng]
        own = "S" + eng if eng in ENGINES else None
        out = []
        for s, v in waits.items():
            if know.get(s, 0) >= v:
                continue
            if s == own and v > self.cnt[s]:
                continue
            out.append((s, v))
        return out

    def _merge_known(self, eng, waits):
        know = self.know[eng]
        for s, v in waits:
            if know.get(s, 0) < v:
                know[s] = v
            snap = self.snaps.get((s, v))
            if snap:
                for s2, v2 in snap.items():
                    if know.get(s2, 0) < v2:
                        know[s2] = v2

    def _apply_tag(self, tag, reads, writes, wappend):
        s, v = tag
        for w in writes:
            w.wd = {s: v}
            w.readers = {}
        for w in wappend:
            if w.wd.get(s, 0) < v:
                w.wd[s] = v
        for r in reads:
            if r.readers.get(s, 0) < v:
                r.readers[s] = v

    def emit(self, eng, fn, reads=(), writes=(), wappend=(), inc=True):
        waits = self._waits_for(eng, reads, writes, wappend)
        og = getattr(self, "_open_grp", None)
        if eng != "tensor":
            for s, v in waits:
                if s == "Stensor" and v > self.cnt["Stensor"]:
                    if og is None:
                        og = self._open_grp = {}
                    og.setdefault("S" + eng, self.cnt["S" + eng] + 1)
        else:
            if og:
                for s, v in waits:
                    if og.get(s) is not None and v >= og[s]:
                        raise RuntimeError(
                            f"DEADLOCK: mid-group tensor op waits {s}>={v} but "
                            f"{s}'s op #{og[s]} waits this open group's end")
        self._merge_known(eng, waits)
        sem = "S" + eng
        if inc:
            self.cnt[sem] += 1
            val = self.cnt[sem]
            self.snaps[(sem, val)] = dict(self.know[eng])
            tag = (sem, val)
            if eng == "tensor":
                self._open_grp = None
        else:
            tag = (sem, self.cnt[sem] + 1)
        if DEBUG_TB:
            fn._tb = "".join(traceback.format_stack(limit=8)[:-2])
        self.ops[eng].append((fn, waits, sem if inc else None))
        self._apply_tag(tag, reads, writes, wappend)
        return tag

    def dma(self, q, fn, reads=(), writes=(), wappend=()):
        i = self.qidx[q] % NSEM_PER_QUEUE
        self.qidx[q] += 1
        sem = self.qsem[q][i]
        throttle = []
        if self.cnt[sem] > 0:
            if self.know[q].get(sem, 0) < self.cnt[sem] * 16:
                throttle = [(sem, self.cnt[sem] * 16)]
        waits = self._waits_for(q, reads, writes, wappend) + throttle
        self._merge_known(q, waits)
        self.cnt[sem] += 1
        val = self.cnt[sem] * 16
        self.snaps[(sem, val)] = dict(self.know[q])
        self.ops[q].append((fn, waits, (sem, 16)))
        self._apply_tag((sem, val), reads, writes, wappend)
        return (sem, val)

    def collective(self, fn, reads=(), writes=()):
        sem = "Dcoll"
        waits = self._waits_for("gpsimd", reads, writes)
        self._merge_known("gpsimd", waits)
        self.cnt[sem] += 1
        val = self.cnt[sem] * 16
        self.snaps[(sem, val)] = dict(self.know["gpsimd"])
        self.ops["gpsimd"].append((fn, waits, (sem, 16)))
        self._apply_tag((sem, val), reads, writes, ())
        return (sem, val)

    def barrier(self):
        tgt = {}
        for s in self.sem_names:
            c = self.cnt[s]
            if c > 0:
                tgt[s] = c * 16 if s[0] == "D" else c
        for e in ENGINES:
            waits = [(s, v) for s, v in tgt.items()
                     if s != "S" + e and self.know[e].get(s, 0) < v]
            self._merge_known(e, waits)
            if waits:
                self.ops[e].append((None, waits, None))
        self.gen += 1

    def const(self, name, arr, dtype):
        if name in self.const_tiles:
            return self.const_tiles[name]
        npdt = np.float32 if dtype == F32 else ml_dtypes.bfloat16
        arr = np.ascontiguousarray(arr.astype(npdt))
        self.const_arrays["c_" + name] = arr
        d = self.dram_in("c_" + name, list(arr.shape), dtype)
        t = Tile(self._es.enter_context(
            self.nc.sbuf_tensor(f"c_{name}_{self.uid()}", list(arr.shape), dtype)),
            "c_" + name)
        self.dma("sync", lambda e, d=d, t=t: e.dma_start(out=t[:], in_=d.ap),
                 reads=[d.r], writes=[t.r])
        self.const_tiles[name] = t
        return t

    def finalize(self):
        nc = self.nc
        cvals = sorted(self.need_consts)
        cpool = {}
        if cvals:
            assert len(cvals) <= self.NCONST, f"const pool too small: {len(cvals)}"
            cp32, cp16 = self._cp32, self._cp16
            for i, v in enumerate(cvals):
                nc.const_aps.register(F32, v, cp32[:, i:i + 1])
                nc.const_aps.register(BF16, v, cp16[:, i:i + 1])
            cpool = {"f32": cp32, "bf16": cp16, "vals": cvals}
        with ExitStack() as es:
            sems = {s: es.enter_context(nc.semaphore(s)) for s in self.sem_names}
            block = es.enter_context(nc.Block())
            for ename in ENGINES:
                ops = self.ops[ename]
                if not ops and not (ename == "vector" and cpool):
                    continue

                def replay(eng, ops=ops, sems=sems, ename=ename):
                    if ename == "vector" and cpool:
                        for i, v in enumerate(cpool["vals"]):
                            eng.memset(cpool["f32"][:, i:i + 1], v)
                            ins = eng.memset(cpool["bf16"][:, i:i + 1], v)
                        ins.then_inc(sems["Sconst"], 1)
                        eng.wait_ge(sems["Sconst"], 1)
                    elif cpool:
                        eng.wait_ge(sems["Sconst"], 1)
                    for fn, waits, inc in ops:
                        for s, v in waits:
                            eng.wait_ge(sems[s], v)
                        if fn is not None:
                            try:
                                inst = fn(eng)
                            except Exception:
                                import traceback as _tb2
                                _tb2.print_exc()
                                if hasattr(fn, "_tb"):
                                    print("EMISSION SITE:\n" + fn._tb)
                                raise
                            if inc is not None:
                                if isinstance(inc, tuple):
                                    inst.then_inc(sems[inc[0]], inc[1])
                                else:
                                    inst.then_inc(sems[inc], 1)
                getattr(block, ename)(replay)
        self._es.close()
        return nc



def ceil_div(a, b):
    return -(-a // b)


class Env:

    def __init__(self, prog, T):
        self.p = prog
        self.T = T
        self.nt = T // 512
        self.ntq = T // 128
        p = prog
        self.eye_bf = p.const("eye_bf", np.eye(128), BF16)
        self.eye_f32 = p.const("eye_f32", np.eye(128), F32)
        self.ones_bf = p.const("ones_bf", np.ones((128, 128)), BF16)
        self.ones_f32 = p.const("ones_f32", np.ones((128, 128)), F32)
        self.pf = [p.psum(f"pf{i}", [128, 512], F32) for i in range(6)]
        self.pb = [p.psum(f"pb{i}", [128, 1024], BF16) for i in range(2)]

    def mm(self, ps_view, lhsT_view, rhs_view, reads, wr, start, stop):
        return self.p.emit(
            "tensor",
            lambda e, o=ps_view, l=lhsT_view, r=rhs_view, st=start, sp=stop:
                e.matmul(o, l, r, start=st, stop=sp),
            reads=reads, writes=[wr], inc=stop)

    def transpose(self, ps_view, in_view, ident_tile, reads, wr):
        np_ = in_view.shape[0]
        return self.p.emit(
            "tensor",
            lambda e, o=ps_view, i=in_view, d=ident_tile, n=np_:
                e.transpose(o, i, d[0:n, 0:n]),
            reads=reads + [ident_tile.r], writes=[wr])

    def act(self, out_view, in_view, func, reads, writes, bias=0.0, scale=1.0,
            accum_out=None):
        if isinstance(bias, float) and bias != 0.0:
            self.p.need_consts.add(bias)
        if isinstance(scale, float) and scale != 1.0:
            self.p.need_consts.add(scale)
        return self.p.emit(
            "scalar",
            lambda e, o=out_view, i=in_view, f=func, b=bias, s=scale, a=accum_out:
                e.activation(o, i, f, bias=b, scale=s, accum_out=a),
            reads=reads, writes=writes)

    def v(self, method, *args, reads=(), writes=(), wappend=(), kwargs=None):
        return self.p.emit(
            "vector",
            lambda e, m=method, a=args, kw=(kwargs or {}): getattr(e, m)(*a, **kw),
            reads=reads, writes=writes, wappend=wappend)

    def g(self, method, *args, reads=(), writes=(), kwargs=None):
        if kwargs and isinstance(kwargs.get("fill"), float) and kwargs["fill"] != 0.0:
            self.p.need_consts.add(kwargs["fill"])

        def _fn(e, m=method, a=args, kw=(kwargs or {})):
            try:
                return getattr(e, m)(*a, **kw)
            except Exception:
                print("GPSIMD OP FAILED:", m, [getattr(x, 'shape', x) for x in a], kw)
                raise
        return self.p.emit("gpsimd", _fn, reads=reads, writes=writes)

    def ld(self, q, dst_view, src_view, reads=(), writes=(), wappend=()):
        return self.p.dma(q, lambda e, d=dst_view, s=src_view: e.dma_start(out=d, in_=s),
                          reads=reads, writes=writes, wappend=wappend)


def emit_ln(env, x_d, out_tiles, col0=0, ncols=None, eps=1e-5, out_dram=None):
    p = env.p
    ncols = ncols if ncols is not None else env.T
    if out_dram is not None:
        ob = [p.sbuf("ln_ob", [128, 512], BF16) for _ in range(4)]
    xb = [p.sbuf("ln_xb", [128, 512], BF16) for _ in range(16)]
    sq = [p.sbuf("ln_sq", [128, 512], BF16) for _ in range(8)]
    m_r = p.sbuf("ln_m", [1, 512], F32)
    msq_r = p.sbuf("ln_msq", [1, 512], F32)
    var_r = p.sbuf("ln_var", [1, 512], F32)
    rstd_r = p.sbuf("ln_rstd", [1, 512], F32)
    mr_r = p.sbuf("ln_mr", [1, 512], F32)
    rb = p.sbuf("ln_rb", [128, 512], F32)
    tmp = [p.sbuf("ln_tmp", [128, 512], F32) for _ in range(2)]
    for tt in range(ncols // 512):
        par = (tt % 2) * 8
        cs = slice(col0 + tt * 512, col0 + (tt + 1) * 512)
        for cc in range(8):
            t = xb[par + cc]
            env.ld("gpsimd", t[:], x_d[cc * 128:(cc + 1) * 128, cs],
                   reads=[x_d.r], writes=[t.r])
        for cc in range(8):
            env.act(sq[cc][:], xb[par + cc][:], AF.Square,
                    reads=[xb[par + cc].r], writes=[sq[cc].r])
        ps0, ps1 = env.pf[0], env.pf[1]
        for cc in range(8):
            env.mm(ps0[0:1, :], env.ones_bf[:, 0:1], xb[par + cc][:],
                   [env.ones_bf.r, xb[par + cc].r], ps0.r, cc == 0, cc == 7)
        for cc in range(8):
            env.mm(ps1[0:1, :], env.ones_bf[:, 0:1], sq[cc][:],
                   [env.ones_bf.r, sq[cc].r], ps1.r, cc == 0, cc == 7)
        p.emit("scalar", lambda e, o=m_r[:], i=ps0[0:1, :]: e.mul(o, i, 1.0 / 1024),
               reads=[ps0.r], writes=[m_r.r])
        env.act(msq_r[:], m_r[:], AF.Square, reads=[m_r.r], writes=[msq_r.r])
        p.emit("scalar", lambda e, o=var_r[:], i=ps1[0:1, :]: e.mul(o, i, 1.0 / 1024),
               reads=[ps1.r], writes=[var_r.r])
        env.v("tensor_sub", var_r[:], var_r[:], msq_r[:],
              reads=[msq_r.r], writes=[var_r.r])
        env.act(var_r[:], var_r[:], AF.Ln, bias=eps, reads=[var_r.r], writes=[var_r.r])
        env.act(rstd_r[:], var_r[:], AF.Exp, scale=-0.5,
                reads=[var_r.r], writes=[rstd_r.r])
        env.v("tensor_mul", mr_r[:], m_r[:], rstd_r[:],
              reads=[m_r.r, rstd_r.r], writes=[mr_r.r])
        ps2, ps3 = env.pf[2], env.pf[3]
        env.mm(ps2[:, :], env.ones_f32[0:1, :], rstd_r[:],
               [env.ones_f32.r, rstd_r.r], ps2.r, True, True)
        env.mm(ps3[:, :], env.ones_f32[0:1, :], mr_r[:],
               [env.ones_f32.r, mr_r.r], ps3.r, True, True)
        env.v("tensor_copy", rb[:], ps2[:, :], reads=[ps2.r], writes=[rb.r])
        for cc in range(8):
            tm = tmp[cc % 2]
            env.v("tensor_mul", tm[:], xb[par + cc][:], rb[:],
                  reads=[xb[par + cc].r, rb.r], writes=[tm.r])
            if out_dram is not None:
                o = ob[cc % 4]
                env.v("tensor_sub", o[:], tm[:], ps3[:, :],
                      reads=[tm.r, ps3.r], writes=[o.r])
                env.ld("sync", out_dram[cc * 128:(cc + 1) * 128, cs], o[:],
                       reads=[o.r], wappend=[out_dram.r])
            else:
                ot = out_tiles[cc]
                env.v("tensor_sub", ot[0:128, cs], tm[:], ps3[:, :],
                      reads=[tm.r, ps3.r], writes=[ot.r])



def emit_l2_ternarize(env, wT_d, tern_d, alpha_d, acol):
    p = env.p
    wt = [p.sbuf("tq_w", [128, 512], F32) for _ in range(2)]
    ab = [p.sbuf("tq_ab", [128, 512], BF16) for _ in range(16)]
    srow = p.sbuf("tq_srow", [1, 1], F32)
    drow = p.sbuf("tq_d", [1, 1], F32)
    dcol = p.sbuf("tq_dc", [128, 1], F32)
    ndcol = p.sbuf("tq_ndc", [128, 1], F32)
    a_sb = p.sbuf("tq_a", [1, 1], F32)
    t1 = p.sbuf("tq_t1", [128, 512], F32)
    t2 = p.sbuf("tq_t2", [128, 512], F32)
    t3 = p.sbuf("tq_t3", [128, 512], BF16)
    ps = env.pf[4]
    n = 0
    for cc in range(8):
        for tb in range(2):
            w = wt[n % 2]
            env.ld("sync", w[:], wT_d[cc * 128:(cc + 1) * 128, tb * 512:(tb + 1) * 512],
                   reads=[wT_d.r], writes=[w.r])
            env.act(ab[n][:], w[:], AF.Abs, reads=[w.r], writes=[ab[n].r])
            env.mm(ps[0:1, :], env.ones_bf[:, 0:1], ab[n][:],
                   [env.ones_bf.r, ab[n].r], ps.r, n == 0, n == 15)
            n += 1
    env.v("reduce_sum", srow[:], ps[0:1, :], AX.X, reads=[ps.r], writes=[srow.r])
    p.emit("scalar", lambda e, o=drow[:], i=srow[:]: e.mul(o, i, 0.7 / (1024.0 * 1024.0)),
           reads=[srow.r], writes=[drow.r])
    ps2 = env.pf[5]
    env.mm(ps2[:, 0:1], env.ones_f32[0:1, :], drow[:], [env.ones_f32.r, drow.r],
           ps2.r, True, True)
    env.v("tensor_copy", dcol[:], ps2[:, 0:1], reads=[ps2.r], writes=[dcol.r])
    p.emit("scalar", lambda e, o=ndcol[:], i=dcol[:]: e.mul(o, i, -1.0),
           reads=[dcol.r], writes=[ndcol.r])
    env.ld("sync", a_sb[:], alpha_d[0:1], reads=[alpha_d.r], writes=[a_sb.r])
    env.mm(ps2[:, 1:2], env.ones_f32[0:1, :], a_sb[:], [env.ones_f32.r, a_sb.r],
           ps2.r, True, True)
    env.v("tensor_copy", acol[:], ps2[:, 1:2], reads=[ps2.r], writes=[acol.r])
    n = 0
    for cc in range(8):
        for tb in range(2):
            w = wt[n % 2]
            env.ld("sync", w[:], wT_d[cc * 128:(cc + 1) * 128, tb * 512:(tb + 1) * 512],
                   reads=[wT_d.r], writes=[w.r])
            env.v("tensor_scalar", t1[:], w[:], dcol[:, 0:1], None, ALU.is_gt,
                  reads=[w.r, dcol.r], writes=[t1.r])
            env.v("tensor_scalar", t2[:], w[:], ndcol[:, 0:1], None, ALU.is_lt,
                  reads=[w.r, ndcol.r], writes=[t2.r])
            env.v("tensor_sub", t3[:], t1[:], t2[:], reads=[t1.r, t2.r], writes=[t3.r])
            p.dma("sync", lambda e, d=tern_d[cc * 128:(cc + 1) * 128,
                                              tb * 512:(tb + 1) * 512], s=t3[:]:
                  e.dma_start(out=d, in_=s), reads=[t3.r], wappend=[tern_d.r])
            n += 1



def emit_l2_ffn(env, x_d, xo_d, tern_d, acol, bias_t, half=0.5):
    p = env.p
    xn_d = p.dram(f"l2f_xn_{p.uid()}", [1024, env.T], BF16)
    emit_ln_dram(env, x_d, xn_d)
    p.open_scope()
    xn = load_fm(env, xn_d, "l2f_xn")
    wt = [p.sbuf("l2f_w", [128, 1024], BF16) for _ in range(8)]
    for k in range(8):
        env.ld("sync", wt[k][:], tern_d[k * 128:(k + 1) * 128, :],
               reads=[tern_d.r], writes=[wt[k].r])
    xcur = [p.sbuf("l2f_x", [128, 512], F32) for _ in range(4)]
    res = [p.sbuf("l2f_r", [128, 512], F32) for _ in range(4)]
    tmp = [p.sbuf("l2f_t", [128, 512], F32) for _ in range(2)]
    n = 0
    for tt in range(env.nt):
        cs = slice(tt * 512, (tt + 1) * 512)
        for o in range(8):
            ps = env.pf[n % 3]
            xc = xcur[n % 4]
            rs = res[n % 4]
            tm = tmp[n % 2]
            env.ld("sync", xc[:], x_d[o * 128:(o + 1) * 128, cs],
                   reads=[x_d.r], writes=[xc.r])
            for k in range(8):
                env.mm(ps[:, :], wt[k][:, o * 128:(o + 1) * 128],
                       xn[k][0:128, cs], [wt[k].r, xn[k].r], ps.r,
                       k == 0, k == 7)
            env.v("tensor_scalar", tm[:], ps[:, :], acol[:, 0:1],
                  bias_t[0:128, o:o + 1], ALU.mult, ALU.add,
                  reads=[ps.r, acol.r, bias_t.r], writes=[tm.r])
            env.v("scalar_tensor_tensor", rs[:], tm[:], half, xc[:],
                  ALU.mult, ALU.add, reads=[tm.r, xc.r], writes=[rs.r])
            p.dma("sync", lambda e, d=xo_d[o * 128:(o + 1) * 128, cs], s=rs[:]:
                  e.dma_start(out=d, in_=s), reads=[rs.r], wappend=[xo_d.r])
            n += 1
    p.close_scope()




class WStream:

    def __init__(self, env, n, cols, name="ws"):
        self.tiles = [env.p.sbuf(name, [128, 8 * cols], BF16) for _ in range(n)]
        self.n = n
        self.i = 0
        self.env = env
        self.cols = cols

    def get(self, src_view, src_r, ncols=None, q="gpsimd"):
        t = self.tiles[self.i % self.n]
        self.i += 1
        nc_ = ncols or self.cols
        self.env.ld(q, t[0:128, 0:nc_], src_view, reads=[src_r], writes=[t.r])
        return t

    def get_og(self, W_d, row0, kn, col0, nc_, q="gpsimd"):
        assert nc_ <= self.cols
        slots = []
        for s in range(ceil_div(kn, 8)):
            t = self.tiles[self.i % self.n]
            self.i += 1
            kk = min(8, kn - s * 8)
            src = W_d[row0 + s * 8 * 128:row0 + (s * 8 + kk) * 128,
                      col0:col0 + nc_]
            src3 = src.rearrange("(k p) c -> p k c", p=128)
            dst3 = t.t.ap()[:, 0:kk * nc_].rearrange("p (k c) -> p k c", k=kk)
            self.env.ld(q, dst3, src3, reads=[W_d.r], writes=[t.r])
            slots.append(t)
        return slots


def emit_linear_fm(env, x_tiles, W_d, outdim, ws, evac, psum_banks=(0, 1),
                   ncols_group=512, w_row0=0, kn=8, col0=0, ncols=None,
                   wq="gpsimd"):
    ncols = ncols if ncols is not None else env.T
    nt = ncols // 512
    og_n = ncols_group // 128
    bi = 0
    for og in range(outdim // ncols_group):
        slots = ws.get_og(W_d, w_row0, kn, og * ncols_group, ncols_group, q=wq)
        for oo in range(og_n):
            o = og * og_n + oo
            for tt in range(nt):
                ps = env.pf[psum_banks[bi % len(psum_banks)]]
                bi += 1
                for k in range(kn):
                    wt = slots[k // 8]
                    wc = (k % 8) * ncols_group + oo * 128
                    env.mm(ps[:, :], wt[:, wc:wc + 128],
                           x_tiles[k][0:128, col0 + tt * 512:col0 + (tt + 1) * 512],
                           [wt.r, x_tiles[k].r], ps.r, k == 0, k == kn - 1)
                evac(o, tt, ps)



def _causal_mask128():
    m = np.zeros((128, 128), np.float32)
    p_, f_ = np.mgrid[0:128, 0:128]
    m[p_ < f_] = -1e30
    return m


def emit_attention(env, q_d, k_d, v_d, ctx_d, causal=False, nheads=16):
    p, T = env.p, env.T
    cmask = p.const("cmask", _causal_mask128(), F32) if causal else None
    nqt = T // 128
    nsup = T // 512
    qh = p.sbuf("at_q", [64, T], BF16)
    kh = p.sbuf("at_k", [64, T], BF16)
    vh = p.sbuf("at_v", [64, T], BF16)
    vtok = [p.sbuf("at_vt", [128, 65], BF16) for _ in range(nqt)]
    pbuf = [p.sbuf("at_p", [128, T], BF16) for _ in range(4)]
    ptt = p.sbuf("at_pt", [128, T * 4], BF16)
    ssc2 = [p.sbuf("at_ssc", [128, T], F32) for _ in range(2)]
    bmax2 = [p.sbuf("at_bm", [128, 1], F32) for _ in range(2)]
    rden = p.sbuf("at_rd", [1, 512], F32)
    rb = p.sbuf("at_rb", [64, 512], F32)
    ctxs = p.sbuf("at_ctx", [64, 512], BF16)
    for kt in range(nqt):
        env.v("memset", vtok[kt][:, 64:65], 1.0, writes=[vtok[kt].r])

    for h in range(nheads):
        hr = slice(h * 64, (h + 1) * 64)
        env.ld("sync", qh[:], q_d[hr, :], reads=[q_d.r], writes=[qh.r])
        env.ld("sync", kh[:], k_d[hr, :], reads=[k_d.r], writes=[kh.r])
        env.ld("sync", vh[:], v_d[hr, :], reads=[v_d.r], writes=[vh.r])
        for g in range(nqt // 4):
            bank = env.pb[g % 2]
            for j in range(4):
                kt = g * 4 + j
                env.transpose(bank[0:128, j * 256:j * 256 + 64],
                              vh[:, kt * 128:(kt + 1) * 128], env.eye_bf,
                              [vh.r], bank.r)
            for j in range(4):
                kt = g * 4 + j
                env.v("tensor_copy", vtok[kt][:, 0:64],
                      bank[0:128, j * 256:j * 256 + 64],
                      reads=[bank.r], writes=[vtok[kt].r])
        for sup in range(nsup):
            sup_kext = min((sup * 4 + 4) * 128, T) if causal else T
            for j in range(4):
                qt = sup * 4 + j
                ssc = ssc2[j % 2]
                bmax = bmax2[j % 2]
                kext = min((qt + 1) * 128, T) if causal else T
                nkb = ceil_div(kext, 512)
                for kb in range(nkb):
                    ke = min(512, kext - kb * 512)
                    ps = env.pf[kb % 2]
                    env.mm(ps[:, 0:ke], qh[:, qt * 128:(qt + 1) * 128],
                           kh[:, kb * 512:kb * 512 + ke], [qh.r, kh.r],
                           ps.r, True, True)
                    env.act(ssc[:, kb * 512:kb * 512 + ke], ps[:, 0:ke],
                            AF.Identity, scale=0.125, reads=[ps.r], writes=[ssc.r])
                if causal:
                    db0 = qt * 128
                    env.v("tensor_add", ssc[:, db0:kext], ssc[:, db0:kext],
                          cmask[:], reads=[cmask.r], writes=[ssc.r])
                env.v("reduce_max", bmax[:], ssc[:, 0:kext], AX.X,
                      reads=[ssc.r], writes=[bmax.r], kwargs=dict(negate=True))
                pb_ = pbuf[j]
                for kb in range(ceil_div(kext, 512)):
                    ke = min(512, kext - kb * 512)
                    env.act(pb_[:, kb * 512:kb * 512 + ke],
                            ssc[:, kb * 512:kb * 512 + ke], AF.Exp,
                            bias=bmax[:, 0:1], reads=[ssc.r, bmax.r], writes=[pb_.r])
                if causal and kext < sup_kext:
                    env.v("memset", pb_[:, kext:sup_kext], 0.0, writes=[pb_.r])
            nkb128 = sup_kext // 128
            ptt3 = ptt.t.ap().rearrange("p (b c) -> p b c", c=512)
            for g in range(ceil_div(nkb128, 8)):
                njj = min(8, nkb128 - g * 8)
                for j in range(4):
                    bank = env.pb[(g * 4 + j) % 2]
                    for jj in range(njj):
                        kt = g * 8 + jj
                        env.transpose(bank[0:128, jj * 128:(jj + 1) * 128],
                                      pbuf[j][:, kt * 128:(kt + 1) * 128],
                                      env.eye_bf, [pbuf[j].r], bank.r)
                    dst = ptt3[:, g * 8:g * 8 + njj, j * 128:(j + 1) * 128]
                    src = bank.t.ap()[:, 0:njj * 128].rearrange(
                        "p (b c) -> p b c", c=128)
                    env.v("tensor_copy", dst, src, reads=[bank.r], writes=[ptt.r])
            psc = env.pf[2 + (sup % 2)]
            for kb in range(nkb128):
                env.mm(psc[0:65, :], vtok[kb][:, 0:65],
                       ptt[:, kb * 512:(kb + 1) * 512], [vtok[kb].r, ptt.r],
                       psc.r, kb == 0, kb == nkb128 - 1)
            env.v("reciprocal", rden[:], psc[64:65, :], reads=[psc.r], writes=[rden.r])
            ps4 = env.pf[4]
            env.mm(ps4[0:64, :], env.ones_f32[0:1, 0:64], rden[:],
                   [env.ones_f32.r, rden.r], ps4.r, True, True)
            env.v("tensor_copy", rb[:], ps4[0:64, :], reads=[ps4.r], writes=[rb.r])
            env.v("tensor_mul", ctxs[:], psc[0:64, :], rb[:],
                  reads=[psc.r, rb.r], writes=[ctxs.r])
            env.ld("sync", ctx_d[hr, sup * 512:(sup + 1) * 512], ctxs[:],
                   reads=[ctxs.r], wappend=[ctx_d.r])



def host_fm(x):
    return np.ascontiguousarray(np.asarray(x).T)


def host_col(v):
    return np.ascontiguousarray(np.asarray(v).reshape(8, 128).T)


def host_T(w):
    return np.ascontiguousarray(np.asarray(w).T)




def emit_ln_dram(env, x_d, xn_d):
    env.p.open_scope()
    emit_ln(env, x_d, None, out_dram=xn_d)
    env.p.close_scope()


def load_fm(env, src_d, name="fmld", n=8):
    p, T = env.p, env.T
    ts = [p.sbuf(name, [128, T], BF16) for _ in range(n)]
    for c in range(n):
        env.ld("sync", ts[c][:], src_d[c * 128:(c + 1) * 128, :],
               reads=[src_d.r], writes=[ts[c].r])
    return ts


def _rot_pt():
    m = np.zeros((128, 128), np.float32)
    for b in (0, 64):
        for i in range(32):
            m[b + i + 32, b + i] = -1.0
            m[b + i, b + i + 32] = 1.0
    return m


def emit_l0_attn(env, x_d, xo_d, w):
    p, T = env.p, env.T
    rotPT = p.const("rotPT", _rot_pt(), BF16)
    qrot_d = p.dram("l0a_q", [1024, T], BF16)
    krot_d = p.dram("l0a_k", [1024, T], BF16)
    v_d = p.dram("l0a_v", [1024, T], BF16)
    ctx_d = p.dram("l0a_ctx", [1024, T], BF16)

    xn_d = p.dram("l0a_xn", [1024, T], BF16)
    emit_ln_dram(env, x_d, xn_d)
    p.open_scope()
    xn = load_fm(env, xn_d, "l0a_xn")
    ws = WStream(env, 3, 512, "l0a_ws")
    cosT = p.sbuf("l0a_cos", [128, T], F32)
    sinT = p.sbuf("l0a_sin", [128, T], F32)
    env.ld("sync", cosT[:], w["cos"].ap, reads=[w["cos"].r], writes=[cosT.r])
    env.ld("sync", sinT[:], w["sin"].ap, reads=[w["sin"].r], writes=[sinT.r])
    bt = {}
    for b in ("bq", "bk", "bv", "bo"):
        bt[b] = p.sbuf("l0a_" + b, [128, 8], F32)
        env.ld("sync", bt[b][:], w[b].ap, reads=[w[b].r], writes=[bt[b].r])
    qf = [p.sbuf("l0a_qf", [128, 512], F32) for _ in range(2)]
    qb = [p.sbuf("l0a_qb", [128, 512], BF16) for _ in range(2)]
    t1 = [p.sbuf("l0a_t1", [128, 512], F32) for _ in range(2)]
    qr = [p.sbuf("l0a_qr", [128, 512], BF16) for _ in range(2)]
    cnt = [0]

    def rope_evac(dst_d, bias):
        def evac(o, tt, ps):
            i = cnt[0] % 2
            cnt[0] += 1
            cs = slice(tt * 512, (tt + 1) * 512)
            env.act(qf[i][:], ps[:, :], AF.Identity, bias=bias[:, o:o + 1],
                    reads=[ps.r, bias.r], writes=[qf[i].r])
            env.v("tensor_copy", qb[i][:], qf[i][:], reads=[qf[i].r], writes=[qb[i].r])
            ps2 = env.pf[4 + i]
            env.mm(ps2[:, :], rotPT[:], qb[i][:], [rotPT.r, qb[i].r], ps2.r,
                   True, True)
            env.v("tensor_mul", t1[i][:], qf[i][:], cosT[:, cs],
                  reads=[qf[i].r, cosT.r], writes=[t1[i].r])
            env.v("tensor_mul", qf[i][:], ps2[:, :], sinT[:, cs],
                  reads=[ps2.r, sinT.r], writes=[qf[i].r])
            env.v("tensor_add", qr[i][:], t1[i][:], qf[i][:],
                  reads=[t1[i].r, qf[i].r], writes=[qr[i].r])
            env.ld("sync", dst_d[o * 128:(o + 1) * 128, cs], qr[i][:],
                   reads=[qr[i].r], wappend=[dst_d.r])
        return evac

    def v_evac(o, tt, ps):
        i = cnt[0] % 2
        cnt[0] += 1
        cs = slice(tt * 512, (tt + 1) * 512)
        env.act(qr[i][:], ps[:, :], AF.Identity, bias=bt["bv"][:, o:o + 1],
                reads=[ps.r, bt["bv"].r], writes=[qr[i].r])
        env.ld("sync", v_d[o * 128:(o + 1) * 128, cs], qr[i][:],
               reads=[qr[i].r], wappend=[v_d.r])

    emit_linear_fm(env, xn, w["Wq"], 1024, ws, rope_evac(qrot_d, bt["bq"]))
    emit_linear_fm(env, xn, w["Wk"], 1024, ws, rope_evac(krot_d, bt["bk"]))
    emit_linear_fm(env, xn, w["Wv"], 1024, ws, v_evac)
    p.close_scope()

    p.open_scope()
    emit_attention(env, qrot_d, krot_d, v_d, ctx_d, causal=False)
    p.close_scope()

    p.open_scope()
    ws2 = WStream(env, 3, 512, "l0a_ws2")
    ct = [p.sbuf("l0a_ct", [128, T], BF16) for _ in range(8)]
    for c in range(8):
        env.ld("sync", ct[c][:], ctx_d[c * 128:(c + 1) * 128, :],
               reads=[ctx_d.r], writes=[ct[c].r])
    xcur = [p.sbuf("l0a_x", [128, 512], F32) for _ in range(4)]
    tmp = [p.sbuf("l0a_tm", [128, 512], F32) for _ in range(2)]
    res = [p.sbuf("l0a_res", [128, 512], F32) for _ in range(4)]
    cnt2 = [0]

    def out_evac(o, tt, ps):
        i = cnt2[0]
        cnt2[0] += 1
        cs = slice(tt * 512, (tt + 1) * 512)
        xc = xcur[i % 4]
        rs = res[i % 4]
        tm = tmp[i % 2]
        env.ld("sync", xc[:], x_d[o * 128:(o + 1) * 128, cs],
               reads=[x_d.r], writes=[xc.r])
        env.v("tensor_scalar", tm[:], ps[:, :], bt2[:, o:o + 1], None, ALU.add,
              reads=[ps.r, bo2.r], writes=[tm.r])
        env.v("tensor_add", rs[:], tm[:], xc[:], reads=[tm.r, xc.r], writes=[rs.r])
        env.ld("sync", xo_d[o * 128:(o + 1) * 128, cs], rs[:],
               reads=[rs.r], wappend=[xo_d.r])

    bo2 = p.sbuf("l0a_bo2", [128, 8], F32)
    bt2 = bo2
    env.ld("sync", bo2[:], w["bo"].ap, reads=[w["bo"].r], writes=[bo2.r])
    emit_linear_fm(env, ct, w["Wo"], 1024, ws2, out_evac)
    p.close_scope()


def emit_rms(env, src, out_tiles, wcol, eps, from_dram=False, out_dram=None,
             src_bf16=False):
    p, T = env.p, env.T
    if out_dram is not None:
        ob = [p.sbuf("rms_ob", [128, 512], BF16) for _ in range(4)]
    xb = [p.sbuf("rms_xb", [128, 512], BF16) for _ in range(16)] if from_dram else None
    sq = [p.sbuf("rms_sq", [128, 512], BF16) for _ in range(8)]
    var_r = p.sbuf("rms_var", [1, 512], F32)
    rstd_r = p.sbuf("rms_rstd", [1, 512], F32)
    rb = p.sbuf("rms_rb", [128, 512], F32)
    tmp = [p.sbuf("rms_tmp", [128, 512], F32) for _ in range(2)]
    for tt in range(T // 512):
        cs = slice(tt * 512, (tt + 1) * 512)
        if from_dram:
            par = (tt % 2) * 8
            srcs = []
            for cc in range(8):
                t = xb[par + cc]
                env.ld("sync" if src_bf16 else "gpsimd", t[:],
                       src[cc * 128:(cc + 1) * 128, cs],
                       reads=[src.r], writes=[t.r])
                srcs.append((t, t[0:128, 0:512]))
        else:
            srcs = [(src[cc], src[cc][0:128, cs]) for cc in range(8)]
        for cc in range(8):
            env.act(sq[cc][:], srcs[cc][1], AF.Square,
                    reads=[srcs[cc][0].r], writes=[sq[cc].r])
        ps1 = env.pf[0]
        for cc in range(8):
            env.mm(ps1[0:1, :], env.ones_bf[:, 0:1], sq[cc][:],
                   [env.ones_bf.r, sq[cc].r], ps1.r, cc == 0, cc == 7)
        env.act(var_r[:], ps1[0:1, :], AF.Ln, bias=eps, scale=1.0 / 1024.0,
                reads=[ps1.r], writes=[var_r.r])
        env.act(rstd_r[:], var_r[:], AF.Exp, scale=-0.5,
                reads=[var_r.r], writes=[rstd_r.r])
        ps2 = env.pf[1]
        env.mm(ps2[:, :], env.ones_f32[0:1, :], rstd_r[:],
               [env.ones_f32.r, rstd_r.r], ps2.r, True, True)
        env.v("tensor_copy", rb[:], ps2[:, :], reads=[ps2.r], writes=[rb.r])
        for cc in range(8):
            tm = tmp[cc % 2]
            env.v("tensor_mul", tm[:], srcs[cc][1], rb[:],
                  reads=[srcs[cc][0].r, rb.r], writes=[tm.r])
            if out_dram is not None:
                o = ob[cc % 4]
                env.v("tensor_scalar", o[:], tm[:], wcol[:, cc:cc + 1], None,
                      ALU.mult, reads=[tm.r, wcol.r], writes=[o.r])
                env.ld("sync", out_dram[cc * 128:(cc + 1) * 128, cs], o[:],
                       reads=[o.r], wappend=[out_dram.r])
            elif wcol is not None:
                env.v("tensor_scalar", out_tiles[cc][0:128, cs], tm[:],
                      wcol[:, cc:cc + 1], None, ALU.mult,
                      reads=[tm.r, wcol.r], writes=[out_tiles[cc].r])
            else:
                env.v("tensor_copy", out_tiles[cc][0:128, cs], tm[:],
                      reads=[tm.r], writes=[out_tiles[cc].r])


def _hm2():
    m = np.zeros((128, 2), np.float32)
    m[0:64, 0] = 1.0
    m[64:128, 1] = 1.0
    return m


def emit_l0_ffn(env, x_d, xo_d, w, half=0.5):
    p, T = env.p, env.T
    EPS = 1.1920929e-07
    q_d = p.dram("l0f_q", [1024, T], BF16)
    k_d = p.dram("l0f_k", [1024, T], BF16)
    v_d = p.dram("l0f_v", [1024, T], BF16)
    ctx_d = p.dram("l0f_ctx", [1024, T], BF16)
    xm_d = p.dram("l0f_xm", [1024, T], F32)
    hm2 = p.const("hm2", _hm2(), BF16)
    bc2 = p.const("bc2", _hm2().T, F32)

    xn_d = p.dram("l0f_xn", [1024, T], BF16)
    emit_ln_dram(env, x_d, xn_d)
    p.open_scope()
    cols = {}
    for n in ("n1", "n2", "qn", "kn"):
        cols[n] = p.sbuf("l0f_" + n, [128, 8] if n in ("n1", "n2") else [128, 1], F32)
        env.ld("sync", cols[n][:], w[n].ap, reads=[w[n].r], writes=[cols[n].r])
    h_t = [p.sbuf("l0f_h", [128, T], BF16) for _ in range(8)]
    emit_rms(env, xn_d, h_t, cols["n1"], EPS, from_dram=True, src_bf16=True)
    ws = WStream(env, 3, 512, "l0f_ws")
    sqb = [p.sbuf("l0f_sqb", [128, 512], BF16) for _ in range(2)]
    s2r = [p.sbuf("l0f_s2", [2, 512], F32) for _ in range(2)]
    rb2 = [p.sbuf("l0f_rb2", [128, 512], F32) for _ in range(2)]
    tq = [p.sbuf("l0f_tq", [128, 512], F32) for _ in range(2)]
    qn_sb = [p.sbuf("l0f_qn", [128, 512], BF16) for _ in range(2)]
    cnt = [0]

    def evac_qk(dst_d, wc):
        def evac(o, tt, ps):
            i = cnt[0] % 2
            cnt[0] += 1
            cs = slice(tt * 512, (tt + 1) * 512)
            env.act(sqb[i][:], ps[:, :], AF.Square, reads=[ps.r], writes=[sqb[i].r])
            ps_s = env.pf[2]
            env.mm(ps_s[0:2, :], hm2[:], sqb[i][:], [hm2.r, sqb[i].r], ps_s.r,
                   True, True)
            env.act(s2r[i][0:2, :], ps_s[0:2, :], AF.Ln, bias=EPS, scale=1.0 / 64.0,
                    reads=[ps_s.r], writes=[s2r[i].r])
            env.act(s2r[i][0:2, :], s2r[i][0:2, :], AF.Exp, scale=-0.5,
                    reads=[s2r[i].r], writes=[s2r[i].r])
            ps_b = env.pf[3]
            env.mm(ps_b[:, :], bc2[:], s2r[i][0:2, :], [bc2.r, s2r[i].r], ps_b.r,
                   True, True)
            env.v("tensor_copy", rb2[i][:], ps_b[:, :], reads=[ps_b.r],
                  writes=[rb2[i].r])
            env.v("tensor_mul", tq[i][:], ps[:, :], rb2[i][:],
                  reads=[ps.r, rb2[i].r], writes=[tq[i].r])
            env.v("tensor_scalar", qn_sb[i][:], tq[i][:], wc[:, 0:1], None,
                  ALU.mult, reads=[tq[i].r, wc.r], writes=[qn_sb[i].r])
            env.ld("sync", dst_d[o * 128:(o + 1) * 128, cs], qn_sb[i][:],
                   reads=[qn_sb[i].r], wappend=[dst_d.r])
        return evac

    def evac_v(o, tt, ps):
        i = cnt[0] % 2
        cnt[0] += 1
        cs = slice(tt * 512, (tt + 1) * 512)
        env.v("tensor_copy", qn_sb[i][:], ps[:, :], reads=[ps.r],
              writes=[qn_sb[i].r])
        env.ld("sync", v_d[o * 128:(o + 1) * 128, cs], qn_sb[i][:],
               reads=[qn_sb[i].r], wappend=[v_d.r])

    emit_linear_fm(env, h_t, w["wqT"], 1024, ws, evac_qk(q_d, cols["qn"]))
    emit_linear_fm(env, h_t, w["wkT"], 1024, ws, evac_qk(k_d, cols["kn"]))
    emit_linear_fm(env, h_t, w["wvT"], 1024, ws, evac_v)
    p.close_scope()

    p.open_scope()
    emit_attention(env, q_d, k_d, v_d, ctx_d, causal=True)
    p.close_scope()

    p.open_scope()
    ws2 = WStream(env, 3, 512, "l0f_ws2")
    ct = [p.sbuf("l0f_ct", [128, T], BF16) for _ in range(8)]
    for c in range(8):
        env.ld("sync", ct[c][:], ctx_d[c * 128:(c + 1) * 128, :],
               reads=[ctx_d.r], writes=[ct[c].r])
    resv = [p.sbuf("l0f_res", [128, 512], F32) for _ in range(4)]
    xnb = [p.sbuf("l0f_xnb", [128, 512], BF16) for _ in range(4)]
    cnt2 = [0]

    def evac_ao(o, tt, ps):
        i = cnt2[0] % 4
        cnt2[0] += 1
        cs = slice(tt * 512, (tt + 1) * 512)
        rs = resv[i]
        xb_ = xnb[i]
        env.ld("sync", xb_[:], xn_d[o * 128:(o + 1) * 128, cs],
               reads=[xn_d.r], writes=[xb_.r])
        if o < 2:
            env.v("tensor_copy", rs[:], xb_[:],
                  reads=[xb_.r, ps.r], writes=[rs.r])
        else:
            env.v("tensor_add", rs[:], ps[:, :], xb_[:],
                  reads=[ps.r, xb_.r], writes=[rs.r])
        env.ld("sync", xm_d[o * 128:(o + 1) * 128, cs], rs[:],
               reads=[rs.r], wappend=[xm_d.r])

    emit_linear_fm(env, ct, w["woT"], 1024, ws2, evac_ao)
    p.close_scope()

    xf_d = p.dram("l0f_xf", [1024, T], BF16)
    p.open_scope()
    n2c = p.sbuf("l0f_n2", [128, 8], F32)
    env.ld("sync", n2c[:], w["n2"].ap, reads=[w["n2"].r], writes=[n2c.r])
    emit_rms(env, xm_d, None, n2c, EPS, from_dram=True, out_dram=xf_d)
    p.close_scope()

    gw_d = p.dram("l0f_gw", [4, T], F32)
    p.open_scope()
    gst = [p.sbuf("l0f_gst", [1, 128], F32) for _ in range(4)]
    gtile = p.sbuf("l0f_gt", [128, 32], BF16)
    for k in range(8):
        env.ld("gpsimd", gtile[:, k * 4:(k + 1) * 4],
               w["gateT"][k * 128:(k + 1) * 128, 0:4],
               reads=[w["gateT"].r], writes=[gtile.r])
    lg = p.sbuf("l0f_lg", [4, 512], F32)
    gtm = p.sbuf("l0f_gtm", [128, 4], F32)
    ge = p.sbuf("l0f_ge", [128, 4], F32)
    gp = p.sbuf("l0f_gp", [128, 4], F32)
    glt = p.sbuf("l0f_glt", [128, 4], F32)
    gm1 = p.sbuf("l0f_gm1", [128, 1], F32)
    gm2 = p.sbuf("l0f_gm2", [128, 1], F32)
    gs = p.sbuf("l0f_gs", [128, 1], F32)
    gwt = p.sbuf("l0f_gwt", [128, 4], F32)

    def evac_gate(o, tt, ps):
        cs0 = tt * 512
        env.v("tensor_copy", lg[0:4, :], ps[0:4, :], reads=[ps.r], writes=[lg.r])
        for qb in range(4):
            pst = env.pf[2]
            env.transpose(pst[0:128, 0:4], lg[0:4, qb * 128:(qb + 1) * 128],
                          env.eye_f32, [lg.r], pst.r)
            env.v("tensor_copy", gtm[:], pst[0:128, 0:4], reads=[pst.r],
                  writes=[gtm.r])
            env.v("reduce_max", gm1[:], gtm[:], AX.X, reads=[gtm.r],
                  writes=[gm1.r], kwargs=dict(negate=True))
            env.act(ge[:], gtm[:], AF.Exp, bias=gm1[:, 0:1],
                    reads=[gtm.r, gm1.r], writes=[ge.r])
            env.v("reduce_sum", gs[:], ge[:], AX.X, reads=[ge.r], writes=[gs.r])
            env.v("reciprocal", gs[:], gs[:], reads=[gs.r], writes=[gs.r])
            env.v("tensor_scalar", gp[:], ge[:], gs[:, 0:1], None, ALU.mult,
                  reads=[ge.r, gs.r], writes=[gp.r])
            env.v("reduce_max", gm1[:], gp[:], AX.X, reads=[gp.r], writes=[gm1.r])
            env.v("tensor_scalar", glt[:], gp[:], gm1[:, 0:1], None, ALU.is_lt,
                  reads=[gp.r, gm1.r], writes=[glt.r])
            env.v("tensor_mul", glt[:], gp[:], glt[:], reads=[gp.r], writes=[glt.r])
            env.v("reduce_max", gm2[:], glt[:], AX.X, reads=[glt.r], writes=[gm2.r])
            env.v("tensor_scalar", glt[:], gp[:], gm2[:, 0:1], None, ALU.is_ge,
                  reads=[gp.r, gm2.r], writes=[glt.r])
            env.v("tensor_add", gm1[:], gm1[:], gm2[:], reads=[gm2.r], writes=[gm1.r])
            env.v("tensor_scalar", gm1[:], gm1[:], 1e-8, None, ALU.add,
                  reads=[gm1.r], writes=[gm1.r])
            env.v("reciprocal", gm1[:], gm1[:], reads=[gm1.r], writes=[gm1.r])
            env.v("tensor_mul", gwt[:], gp[:], glt[:], reads=[gp.r, glt.r],
                  writes=[gwt.r])
            env.v("tensor_scalar", gwt[:], gwt[:], gm1[:, 0:1], None, ALU.mult,
                  reads=[gm1.r], writes=[gwt.r])
            for e in range(4):
                pse = env.pf[3]
                env.transpose(pse[0:1, 0:128], gwt[:, e:e + 1], env.eye_f32,
                              [gwt.r], pse.r)
                env.v("tensor_copy", gst[e][:], pse[0:1, 0:128],
                      reads=[pse.r], writes=[gst[e].r])
                env.ld("sync", gw_d[e:e + 1, cs0 + qb * 128:cs0 + (qb + 1) * 128],
                       gst[e][:], reads=[gst[e].r], wappend=[gw_d.r])

    xfb = [p.sbuf("l0f_xfb", [128, 512], BF16) for _ in range(16)]
    for tt in range(T // 512):
        par = (tt % 2) * 8
        for k in range(8):
            env.ld("sync", xfb[par + k][:],
                   xf_d[k * 128:(k + 1) * 128, tt * 512:(tt + 1) * 512],
                   reads=[xf_d.r], writes=[xfb[par + k].r])
        psg = env.pf[0]
        for k in range(8):
            env.mm(psg[0:4, :], gtile[:, k * 4:(k + 1) * 4],
                   xfb[par + k][:], [gtile.r, xfb[par + k].r], psg.r,
                   k == 0, k == 7)
        evac_gate(0, tt, psg)
    p.close_scope()

    TM = min(1024, T)
    p.open_scope()
    wsx = WStream(env, 5, 512, "l0f_wsx")
    xft = [p.sbuf("l0f_xft", [128, TM], BF16) for _ in range(8)]
    growt = [p.sbuf("l0f_grow", [1, 512], F32) for _ in range(2)]
    hA = [p.sbuf("l0f_hA", [128, TM], BF16) for _ in range(16)]
    hB = [p.sbuf("l0f_hB", [128, TM], BF16) for _ in range(16)]
    acc = [p.sbuf("l0f_acc", [128, TM], F32) for _ in range(8)]
    wb = [p.sbuf("l0f_wb", [128, TM], F32) for _ in range(4)]
    xcr = [p.sbuf("l0f_xc", [128, TM], F32) for _ in range(2)]
    xmr = [p.sbuf("l0f_xm", [128, TM], F32) for _ in range(2)]
    pre = p.sbuf("l0f_pre", [128, TM], F32)
    out_sb = [p.sbuf("l0f_out", [128, 512], F32) for _ in range(4)]

    def dn_evac(e):
        def evac(o, tt, ps):
            if e == 0:
                env.v("tensor_copy", acc[o][0:128, tt * 512:(tt + 1) * 512],
                      ps[:, :], reads=[ps.r], writes=[acc[o].r])
            else:
                env.v("tensor_add", acc[o][0:128, tt * 512:(tt + 1) * 512],
                      ps[:, :], acc[o][0:128, tt * 512:(tt + 1) * 512],
                      reads=[ps.r], writes=[acc[o].r])
        return evac

    def h_evac(dst, func, cnt3=[0]):
        def evac(o, tt, ps):
            env.act(dst[o][0:128, tt * 512:(tt + 1) * 512], ps[:, :], func,
                    reads=[ps.r], writes=[dst[o].r])
        return evac

    for tm in range(T // TM):
        c0 = tm * TM
        cs = slice(c0, c0 + TM)
        for k in range(8):
            env.ld("sync", xft[k][:], xf_d[k * 128:(k + 1) * 128, cs],
                   reads=[xf_d.r], writes=[xft[k].r])
        for e in range(4):
            for n2 in range(TM // 512):
                grow = growt[(e * 2 + n2) % 2]
                env.ld("sync", grow[:],
                       gw_d[e:e + 1, c0 + n2 * 512:c0 + (n2 + 1) * 512],
                       reads=[gw_d.r], writes=[grow.r])
                psb = env.pf[4]
                env.mm(psb[:, :], env.ones_f32[0:1, :], grow[:],
                       [env.ones_f32.r, grow.r], psb.r, True, True)
                env.v("tensor_copy", wb[e][0:128, n2 * 512:(n2 + 1) * 512],
                      psb[:, :], reads=[psb.r], writes=[wb[e].r])
        emit_linear_fm(env, xft, w["e0_upT"], 2048, wsx, h_evac(hA, AF.Gelu),
                       col0=0, ncols=TM)
        for hc in range(16):
            env.v("tensor_mul", hA[hc][:], hA[hc][:], wb[0][:],
                  reads=[wb[0].r], writes=[hA[hc].r])
        emit_linear_fm(env, hA, w["e0_downT"], 1024, wsx, dn_evac(0), kn=16,
                       col0=0, ncols=TM, psum_banks=(0, 1))
        emit_linear_fm(env, xft, w["e1_w1T"], 2048, wsx, h_evac(hA, AF.Silu),
                       col0=0, ncols=TM)
        emit_linear_fm(env, xft, w["e1_w2T"], 2048, wsx, h_evac(hB, AF.Identity),
                       col0=0, ncols=TM)
        for hc in range(16):
            env.v("tensor_mul", hA[hc][:], hA[hc][:], hB[hc][:],
                  reads=[hB[hc].r], writes=[hA[hc].r])
            env.v("tensor_mul", hA[hc][:], hA[hc][:], wb[1][:],
                  reads=[wb[1].r], writes=[hA[hc].r])
        emit_linear_fm(env, hA, w["e1_downT"], 1024, wsx, dn_evac(1), kn=16,
                       col0=0, ncols=TM, psum_banks=(0, 1))
        emit_linear_fm(env, xft, w["e2_l1T"], 2048, wsx, h_evac(hA, AF.Gelu),
                       col0=0, ncols=TM)
        emit_linear_fm(env, hA, w["e2_l2T"], 2048, wsx, h_evac(hB, AF.Gelu),
                       kn=16, col0=0, ncols=TM)
        emit_linear_fm(env, hB, w["e2_l3T"], 2048, wsx, h_evac(hA, AF.Gelu),
                       kn=16, col0=0, ncols=TM)
        for hc in range(16):
            env.v("tensor_mul", hA[hc][:], hA[hc][:], wb[2][:],
                  reads=[wb[2].r], writes=[hA[hc].r])
        emit_linear_fm(env, hA, w["e2_l4T"], 1024, wsx, dn_evac(2), kn=16,
                       col0=0, ncols=TM, psum_banks=(0, 1))
        emit_linear_fm(env, xft, w["e3_downT"], 1024, wsx,
                       h_evac(hB, AF.Gelu), col0=0, ncols=TM)
        emit_linear_fm(env, hB[0:8], w["e3_upT"], 2048, wsx, h_evac(hA, AF.Gelu),
                       kn=8, col0=0, ncols=TM)
        for hc in range(16):
            env.v("tensor_mul", hA[hc][:], hA[hc][:], wb[3][:],
                  reads=[wb[3].r], writes=[hA[hc].r])
        emit_linear_fm(env, hA, w["e3_outT"], 1024, wsx, dn_evac(3), kn=16,
                       col0=0, ncols=TM, psum_banks=(0, 1))
        for o in range(8):
            xc, xm = xcr[o % 2], xmr[o % 2]
            env.ld("sync", xc[:], x_d[o * 128:(o + 1) * 128, cs],
                   reads=[x_d.r], writes=[xc.r])
            env.ld("sync", xm[:], xm_d[o * 128:(o + 1) * 128, cs],
                   reads=[xm_d.r], writes=[xm.r])
            env.v("scalar_tensor_tensor", pre[:], xm[:], half, xc[:],
                  ALU.mult, ALU.add, reads=[xm.r, xc.r], writes=[pre.r])
            for n2 in range(TM // 512):
                ob = out_sb[(o * 2 + n2) % 4]
                if o < 2:
                    env.v("tensor_copy", ob[:], pre[:, n2 * 512:(n2 + 1) * 512],
                          reads=[pre.r], writes=[ob.r])
                else:
                    env.v("scalar_tensor_tensor", ob[:],
                          acc[o][0:128, n2 * 512:(n2 + 1) * 512], half,
                          pre[:, n2 * 512:(n2 + 1) * 512], ALU.mult, ALU.add,
                          reads=[acc[o].r, pre.r], writes=[ob.r])
                env.ld("sync", xo_d[o * 128:(o + 1) * 128,
                                    c0 + n2 * 512:c0 + (n2 + 1) * 512], ob[:],
                       reads=[ob.r], wappend=[xo_d.r])
    p.close_scope()


class DView:

    def __init__(self, dram, idx):
        self._ap = dram.ap[idx]
        self.r = dram.r

    def __getitem__(self, s):
        return self._ap[s]


def emit_topk_gate(env, xn_d, routerT_d, E, gw_d, ebias_d=None):
    p, T = env.p, env.T
    p.open_scope()
    gtile = p.sbuf("gt_w", [128, 8 * E], BF16)
    for k in range(8):
        env.ld("gpsimd", gtile[:, k * E:(k + 1) * E],
               routerT_d[k * 128:(k + 1) * 128, 0:E],
               reads=[routerT_d.r], writes=[gtile.r])
    eb = None
    if ebias_d is not None:
        eb = p.sbuf("gt_eb", [E, 1], F32)
        env.ld("sync", eb[:], ebias_d.ap, reads=[ebias_d.r], writes=[eb.r])
    xnb = [p.sbuf("gt_xnb", [128, 512], BF16) for _ in range(16)]
    lg = p.sbuf("gt_lg", [E, 512], F32)
    gtm = p.sbuf("gt_gtm", [128, E], F32)
    nm1 = p.sbuf("gt_nm1", [128, 1], F32)
    m1 = p.sbuf("gt_m1", [128, 1], F32)
    m2 = p.sbuf("gt_m2", [128, 1], F32)
    e1 = p.sbuf("gt_e1", [128, E], F32)
    lt = p.sbuf("gt_lt", [128, E], F32)
    msk = p.sbuf("gt_msk", [128, E], F32)
    gs = p.sbuf("gt_gs", [128, 1], F32)
    gwt = p.sbuf("gt_gwt", [128, E], F32)
    stg = p.sbuf("gt_stg", [E, 128], F32)
    for tt in range(T // 512):
        par = (tt % 2) * 8
        for k in range(8):
            env.ld("sync", xnb[par + k][:],
                   xn_d[k * 128:(k + 1) * 128, tt * 512:(tt + 1) * 512],
                   reads=[xn_d.r], writes=[xnb[par + k].r])
        psg = env.pf[0]
        for k in range(8):
            env.mm(psg[0:E, :], gtile[:, k * E:(k + 1) * E], xnb[par + k][:],
                   [gtile.r, xnb[par + k].r], psg.r, k == 0, k == 7)
        if eb is not None:
            env.v("tensor_scalar", lg[:], psg[0:E, :], eb[0:E, 0:1], None,
                  ALU.add, reads=[psg.r, eb.r], writes=[lg.r])
        else:
            env.v("tensor_copy", lg[:], psg[0:E, :], reads=[psg.r], writes=[lg.r])
        for qb in range(4):
            pst = env.pf[2]
            env.transpose(pst[0:128, 0:E], lg[0:E, qb * 128:(qb + 1) * 128],
                          env.eye_f32, [lg.r], pst.r)
            env.v("tensor_copy", gtm[:], pst[0:128, 0:E], reads=[pst.r],
                  writes=[gtm.r])
            env.v("reduce_max", m1[:], gtm[:], AX.X, reads=[gtm.r], writes=[m1.r])
            env.v("tensor_scalar", nm1[:], m1[:], -1.0, None, ALU.mult,
                  reads=[m1.r], writes=[nm1.r])
            env.act(e1[:], gtm[:], AF.Exp, bias=nm1[:, 0:1],
                    reads=[gtm.r, nm1.r], writes=[e1.r])
            env.v("tensor_scalar", lt[:], gtm[:], m1[:, 0:1], None, ALU.is_lt,
                  reads=[gtm.r, m1.r], writes=[lt.r])
            env.v("tensor_mul", msk[:], gtm[:], lt[:],
                  reads=[gtm.r, lt.r], writes=[msk.r])
            env.v("tensor_scalar", lt[:], lt[:], 1.0, 1e30, ALU.subtract,
                  ALU.mult, writes=[lt.r])
            env.v("tensor_add", msk[:], msk[:], lt[:], reads=[lt.r], writes=[msk.r])
            env.v("reduce_max", m2[:], msk[:], AX.X, reads=[msk.r], writes=[m2.r])
            env.v("tensor_scalar", lt[:], gtm[:], m2[:, 0:1], None, ALU.is_ge,
                  reads=[gtm.r, m2.r], writes=[lt.r])
            env.v("tensor_mul", e1[:], e1[:], lt[:], reads=[lt.r], writes=[e1.r])
            env.v("reduce_sum", gs[:], e1[:], AX.X, reads=[e1.r], writes=[gs.r])
            env.v("reciprocal", gs[:], gs[:], writes=[gs.r])
            env.v("tensor_scalar", gwt[:], e1[:], gs[:, 0:1], None, ALU.mult,
                  reads=[e1.r, gs.r], writes=[gwt.r])
            pse = env.pf[3]
            env.transpose(pse[0:E, 0:128], gwt[:, 0:E], env.eye_f32,
                          [gwt.r], pse.r)
            env.v("tensor_copy", stg[:], pse[0:E, 0:128], reads=[pse.r],
                  writes=[stg.r])
            env.ld("sync", gw_d[0:E, tt * 512 + qb * 128:tt * 512 + (qb + 1) * 128],
                   stg[:], reads=[stg.r], wappend=[gw_d.r])
    p.close_scope()



def emit_l1_ffn(env, x_d, xo_d, w, half=0.5):
    p, T = env.p, env.T
    xn_d = p.dram(f"l1f_xn_{p.uid()}", [1024, T], BF16)
    emit_ln_dram(env, x_d, xn_d)
    gw_d = p.dram(f"l1f_gw_{p.uid()}", [8, T], F32)
    emit_topk_gate(env, xn_d, w["routerT"], 8, gw_d, ebias_d=w["ebias"])

    TM = min(1024, T)
    p.open_scope()
    wsx = WStream(env, 5, 512, "l1f_wsx")
    xft = [p.sbuf("l1f_xft", [128, TM], BF16) for _ in range(8)]
    hA = [p.sbuf("l1f_hA", [128, TM], BF16) for _ in range(16)]
    hB = [p.sbuf("l1f_hB", [128, TM], BF16) for _ in range(16)]
    acc = [p.sbuf("l1f_acc", [128, TM], F32) for _ in range(8)]
    wb = p.sbuf("l1f_wb", [128, TM], F32)
    grow = [p.sbuf("l1f_grow", [1, 512], F32) for _ in range(2)]
    xcr = [p.sbuf("l1f_xc", [128, TM], F32) for _ in range(2)]
    out_sb = [p.sbuf("l1f_out", [128, 512], F32) for _ in range(4)]

    def dn_evac(first):
        def evac(o, tt, ps):
            if first:
                env.v("tensor_copy", acc[o][0:128, tt * 512:(tt + 1) * 512],
                      ps[:, :], reads=[ps.r], writes=[acc[o].r])
            else:
                env.v("tensor_add", acc[o][0:128, tt * 512:(tt + 1) * 512],
                      ps[:, :], acc[o][0:128, tt * 512:(tt + 1) * 512],
                      reads=[ps.r], writes=[acc[o].r])
        return evac

    def h_evac(dst, func):
        def evac(o, tt, ps):
            env.act(dst[o][0:128, tt * 512:(tt + 1) * 512], ps[:, :], func,
                    reads=[ps.r], writes=[dst[o].r])
        return evac

    for tm in range(T // TM):
        c0 = tm * TM
        cs = slice(c0, c0 + TM)
        for k in range(8):
            env.ld("sync", xft[k][:], xn_d[k * 128:(k + 1) * 128, cs],
                   reads=[xn_d.r], writes=[xft[k].r])
        for e in range(10):
            shared = e >= 8
            if shared:
                w1 = DView(w["sw1T"], e - 8)
                w3 = DView(w["sw3T"], e - 8)
                w2 = DView(w["sw2T"], e - 8)
            else:
                w1 = DView(w["w1T"], e)
                w3 = DView(w["w3T"], e)
                w2 = DView(w["w2T"], e)
                for n2 in range(TM // 512):
                    gr = grow[n2 % 2]
                    env.ld("sync", gr[:],
                           gw_d[e:e + 1, c0 + n2 * 512:c0 + (n2 + 1) * 512],
                           reads=[gw_d.r], writes=[gr.r])
                    psb = env.pf[4]
                    env.mm(psb[:, :], env.ones_f32[0:1, :], gr[:],
                           [env.ones_f32.r, gr.r], psb.r, True, True)
                    env.v("tensor_copy", wb[0:128, n2 * 512:(n2 + 1) * 512],
                          psb[:, :], reads=[psb.r], writes=[wb.r])
            emit_linear_fm(env, xft, w1, 2048, wsx, h_evac(hA, AF.Silu),
                           col0=0, ncols=TM)
            emit_linear_fm(env, xft, w3, 2048, wsx, h_evac(hB, AF.Identity),
                           col0=0, ncols=TM)
            for hc in range(16):
                env.v("tensor_mul", hA[hc][:], hA[hc][:], hB[hc][:],
                      reads=[hB[hc].r], writes=[hA[hc].r])
                if not shared:
                    env.v("tensor_mul", hA[hc][:], hA[hc][:], wb[:],
                          reads=[wb.r], writes=[hA[hc].r])
            emit_linear_fm(env, hA, w2, 1024, wsx, dn_evac(e == 0), kn=16,
                           col0=0, ncols=TM, psum_banks=(0, 1))
        for o in range(8):
            xc = xcr[o % 2]
            env.ld("sync", xc[:], x_d[o * 128:(o + 1) * 128, cs],
                   reads=[x_d.r], writes=[xc.r])
            for n2 in range(TM // 512):
                ob = out_sb[(o * 2 + n2) % 4]
                env.v("scalar_tensor_tensor", ob[:],
                      acc[o][0:128, n2 * 512:(n2 + 1) * 512], half,
                      xc[:, n2 * 512:(n2 + 1) * 512], ALU.mult, ALU.add,
                      reads=[acc[o].r, xc.r], writes=[ob.r])
                env.ld("sync", xo_d[o * 128:(o + 1) * 128,
                                    c0 + n2 * 512:c0 + (n2 + 1) * 512], ob[:],
                       reads=[ob.r], wappend=[xo_d.r])
    p.close_scope()



def emit_l3_ffn(env, x_d, xo_d, w, half=0.5):
    p, T = env.p, env.T
    xn_d = p.dram(f"l3f_xn_{p.uid()}", [1024, T], BF16)
    emit_ln_dram(env, x_d, xn_d)
    gw_d = p.dram(f"l3f_gw_{p.uid()}", [8, T], F32)
    emit_topk_gate(env, xn_d, w["routerT"], 8, gw_d)

    TM = min(1024, T)
    p.open_scope()
    wsx = WStream(env, 6, 512, "l3f_wsx")
    xft = [p.sbuf("l3f_xft", [128, TM], BF16) for _ in range(8)]
    hA = [p.sbuf("l3f_hA", [128, TM], BF16) for _ in range(32)]
    acc = [p.sbuf("l3f_acc", [128, TM], F32) for _ in range(8)]
    wb = p.sbuf("l3f_wb", [128, TM], F32)
    grow = [p.sbuf("l3f_grow", [1, 512], F32) for _ in range(2)]
    gfm = [p.sbuf("l3f_gfm", [8, 512], BF16) for _ in range(2)]
    b1c = p.sbuf("l3f_b1", [128, 32], F32)
    b2t = p.sbuf("l3f_b2", [8, 1024], BF16)
    xcr = [p.sbuf("l3f_xc", [128, TM], F32) for _ in range(2)]
    out_sb = [p.sbuf("l3f_out", [128, 512], F32) for _ in range(4)]
    env.ld("gpsimd", b2t[:], w["b2"].ap, reads=[w["b2"].r], writes=[b2t.r])

    def dn_evac(first):
        def evac(o, tt, ps):
            if first:
                env.v("tensor_copy", acc[o][0:128, tt * 512:(tt + 1) * 512],
                      ps[:, :], reads=[ps.r], writes=[acc[o].r])
            else:
                env.v("tensor_add", acc[o][0:128, tt * 512:(tt + 1) * 512],
                      ps[:, :], acc[o][0:128, tt * 512:(tt + 1) * 512],
                      reads=[ps.r], writes=[acc[o].r])
        return evac

    for tm in range(T // TM):
        c0 = tm * TM
        cs = slice(c0, c0 + TM)
        for k in range(8):
            env.ld("sync", xft[k][:], xn_d[k * 128:(k + 1) * 128, cs],
                   reads=[xn_d.r], writes=[xft[k].r])
        for e in range(8):
            env.ld("sync", b1c[:], DView(w["b1"], e)[0:128, 0:32],
                   reads=[w["b1"].r], writes=[b1c.r])
            for n2 in range(TM // 512):
                gr = grow[n2 % 2]
                env.ld("sync", gr[:],
                       gw_d[e:e + 1, c0 + n2 * 512:c0 + (n2 + 1) * 512],
                       reads=[gw_d.r], writes=[gr.r])
                psb = env.pf[4]
                env.mm(psb[:, :], env.ones_f32[0:1, :], gr[:],
                       [env.ones_f32.r, gr.r], psb.r, True, True)
                env.v("tensor_copy", wb[0:128, n2 * 512:(n2 + 1) * 512],
                      psb[:, :], reads=[psb.r], writes=[wb.r])

            def h1_evac(o, tt, ps):
                env.act(hA[o][0:128, tt * 512:(tt + 1) * 512], ps[:, :],
                        AF.Gelu, bias=b1c[:, o:o + 1],
                        reads=[ps.r, b1c.r], writes=[hA[o].r])

            emit_linear_fm(env, xft, DView(w["w1T"], e), 4096, wsx, h1_evac,
                           col0=0, ncols=TM)
            for hc in range(32):
                env.v("tensor_mul", hA[hc][:], hA[hc][:], wb[:],
                      reads=[wb.r], writes=[hA[hc].r])
            emit_linear_fm(env, hA, DView(w["w2T"], e), 1024, wsx,
                           dn_evac(e == 0), kn=32, col0=0, ncols=TM,
                           psum_banks=(0, 1))
        for n2 in range(TM // 512):
            g8 = gfm[n2 % 2]
            env.ld("gpsimd", g8[:], gw_d[0:8, c0 + n2 * 512:c0 + (n2 + 1) * 512],
                   reads=[gw_d.r], writes=[g8.r])
            for o in range(8):
                psb = env.pf[2 + (o % 2)]
                env.mm(psb[:, :], b2t[0:8, o * 128:(o + 1) * 128], g8[:],
                       [b2t.r, g8.r], psb.r, True, True)
                env.v("tensor_add", acc[o][0:128, n2 * 512:(n2 + 1) * 512],
                      psb[:, :], acc[o][0:128, n2 * 512:(n2 + 1) * 512],
                      reads=[psb.r], writes=[acc[o].r])
        for o in range(8):
            xc = xcr[o % 2]
            env.ld("sync", xc[:], x_d[o * 128:(o + 1) * 128, cs],
                   reads=[x_d.r], writes=[xc.r])
            for n2 in range(TM // 512):
                ob = out_sb[(o * 2 + n2) % 4]
                env.v("scalar_tensor_tensor", ob[:],
                      acc[o][0:128, n2 * 512:(n2 + 1) * 512], half,
                      xc[:, n2 * 512:(n2 + 1) * 512], ALU.mult, ALU.add,
                      reads=[acc[o].r, xc.r], writes=[ob.r])
                env.ld("sync", xo_d[o * 128:(o + 1) * 128,
                                    c0 + n2 * 512:c0 + (n2 + 1) * 512], ob[:],
                       reads=[ob.r], wappend=[xo_d.r])
    p.close_scope()


def emit_l3_attn(env, x_d, xo_d, w):
    p, T = env.p, env.T
    xn_d = p.dram("l3a_xn", [1024, T], BF16)
    emit_ln_dram(env, x_d, xn_d)
    p.open_scope()
    xt = load_fm(env, xn_d, "l3a_xt")
    wq_sb = p.sbuf("l3a_wq", [128, 512], BF16)
    wk_sb = p.sbuf("l3a_wk", [128, 512], BF16)
    wv_sb = p.sbuf("l3a_wv", [128, 512], BF16)
    wo_sb = p.sbuf("l3a_wo", [64, 1024], BF16)
    for k in range(8):
        env.ld("gpsimd", wq_sb[:, k * 64:(k + 1) * 64],
               w["wqT"][k * 128:(k + 1) * 128, :], reads=[w["wqT"].r],
               writes=[wq_sb.r])
        env.ld("gpsimd", wk_sb[:, k * 64:(k + 1) * 64],
               w["wkT"][k * 128:(k + 1) * 128, :], reads=[w["wkT"].r],
               writes=[wk_sb.r])
        env.ld("gpsimd", wv_sb[:, k * 64:(k + 1) * 64],
               w["wvT"][k * 128:(k + 1) * 128, :], reads=[w["wvT"].r],
               writes=[wv_sb.r])
    env.ld("gpsimd", wo_sb[:], w["woT"].ap, reads=[w["woT"].r], writes=[wo_sb.r])
    qf = p.sbuf("l3a_qf", [64, T], BF16)
    ktm = p.sbuf("l3a_ktm", [128, 64 * (T // 128)], BF16)
    vtm = p.sbuf("l3a_vtm", [128, 64 * (T // 128)], BF16)
    e_sb = p.sbuf("l3a_e", [64, 64], BF16)
    ctx = p.sbuf("l3a_ctx", [64, T], BF16)
    xcr = [p.sbuf("l3a_xc", [128, 512], F32) for _ in range(2)]
    ob = [p.sbuf("l3a_ob", [128, 512], F32) for _ in range(2)]
    for tt in range(T // 512):
        psq = env.pf[tt % 2]
        for k in range(8):
            env.mm(psq[0:64, :], wq_sb[:, k * 64:(k + 1) * 64],
                   xt[k][0:128, tt * 512:(tt + 1) * 512], [wq_sb.r, xt[k].r],
                   psq.r, k == 0, k == 7)
        env.v("tensor_copy", qf[:, tt * 512:(tt + 1) * 512], psq[0:64, :],
              reads=[psq.r], writes=[qf.r])
    for qt in range(T // 128):
        for dst, wsb in ((ktm, wk_sb), (vtm, wv_sb)):
            ps = env.pf[2 + (qt % 2)]
            for k in range(8):
                env.mm(ps[0:128, 0:64], xt[k][0:128, qt * 128:(qt + 1) * 128],
                       wsb[:, k * 64:(k + 1) * 64], [xt[k].r, wsb.r], ps.r,
                       k == 0, k == 7)
            env.v("tensor_copy", dst[:, qt * 64:(qt + 1) * 64], ps[0:128, 0:64],
                  reads=[ps.r], writes=[dst.r])
    pse = env.pf[4]
    for qt in range(T // 128):
        env.mm(pse[0:64, 0:64], ktm[:, qt * 64:(qt + 1) * 64],
               vtm[:, qt * 64:(qt + 1) * 64], [ktm.r, vtm.r], pse.r,
               qt == 0, qt == T // 128 - 1)
    env.act(e_sb[:], pse[0:64, 0:64], AF.Identity, scale=float(math.sqrt(3.0 / 64.0)),
            reads=[pse.r], writes=[e_sb.r])
    for tt in range(T // 512):
        psc = env.pf[tt % 2]
        env.mm(psc[0:64, :], e_sb[:], qf[:, tt * 512:(tt + 1) * 512],
               [e_sb.r, qf.r], psc.r, True, True)
        env.v("tensor_copy", ctx[:, tt * 512:(tt + 1) * 512], psc[0:64, :],
              reads=[psc.r], writes=[ctx.r])
    for o in range(8):
        for tt in range(T // 512):
            ps = env.pf[2 + (tt % 2)]
            env.mm(ps[:, :], wo_sb[:, o * 128:(o + 1) * 128],
                   ctx[:, tt * 512:(tt + 1) * 512], [wo_sb.r, ctx.r], ps.r,
                   True, True)
            xc = xcr[tt % 2]
            env.ld("sync", xc[:], x_d[o * 128:(o + 1) * 128, tt * 512:(tt + 1) * 512],
                   reads=[x_d.r], writes=[xc.r])
            o_ = ob[tt % 2]
            env.v("tensor_add", o_[:], ps[:, :], xc[:], reads=[ps.r, xc.r],
                  writes=[o_.r])
            env.ld("sync", xo_d[o * 128:(o + 1) * 128, tt * 512:(tt + 1) * 512],
                   o_[:], reads=[o_.r], wappend=[xo_d.r])
    p.close_scope()


def emit_lnx(env, src_d, eps, gamma_col=None, beta_col=None, add_dram=None,
             out_f32=None, out_bf=None):
    p, T = env.p, env.T
    p.open_scope()
    xb = [p.sbuf("lnx_xb", [128, 512], BF16) for _ in range(16)]
    sq = [p.sbuf("lnx_sq", [128, 512], BF16) for _ in range(8)]
    ad = [p.sbuf("lnx_ad", [128, 512], F32) for _ in range(2)]
    m_r = p.sbuf("lnx_m", [1, 512], F32)
    msq_r = p.sbuf("lnx_msq", [1, 512], F32)
    var_r = p.sbuf("lnx_var", [1, 512], F32)
    rstd_r = p.sbuf("lnx_rstd", [1, 512], F32)
    mr_r = p.sbuf("lnx_mr", [1, 512], F32)
    rb = p.sbuf("lnx_rb", [128, 512], F32)
    tmp = [p.sbuf("lnx_tmp", [128, 512], F32) for _ in range(2)]
    ob = [p.sbuf("lnx_ob", [128, 512], F32) for _ in range(2)]
    obf = [p.sbuf("lnx_obf", [128, 512], BF16) for _ in range(2)]
    for tt in range(T // 512):
        par = (tt % 2) * 8
        cs = slice(tt * 512, (tt + 1) * 512)
        for cc in range(8):
            t = xb[par + cc]
            env.ld("gpsimd", t[:], src_d[cc * 128:(cc + 1) * 128, cs],
                   reads=[src_d.r], writes=[t.r])
        for cc in range(8):
            env.act(sq[cc][:], xb[par + cc][:], AF.Square,
                    reads=[xb[par + cc].r], writes=[sq[cc].r])
        ps0, ps1 = env.pf[0], env.pf[1]
        for cc in range(8):
            env.mm(ps0[0:1, :], env.ones_bf[:, 0:1], xb[par + cc][:],
                   [env.ones_bf.r, xb[par + cc].r], ps0.r, cc == 0, cc == 7)
        for cc in range(8):
            env.mm(ps1[0:1, :], env.ones_bf[:, 0:1], sq[cc][:],
                   [env.ones_bf.r, sq[cc].r], ps1.r, cc == 0, cc == 7)
        p.emit("scalar", lambda e, o=m_r[:], i=ps0[0:1, :]: e.mul(o, i, 1.0 / 1024),
               reads=[ps0.r], writes=[m_r.r])
        env.act(msq_r[:], m_r[:], AF.Square, reads=[m_r.r], writes=[msq_r.r])
        p.emit("scalar", lambda e, o=var_r[:], i=ps1[0:1, :]: e.mul(o, i, 1.0 / 1024),
               reads=[ps1.r], writes=[var_r.r])
        env.v("tensor_sub", var_r[:], var_r[:], msq_r[:],
              reads=[msq_r.r], writes=[var_r.r])
        env.act(var_r[:], var_r[:], AF.Ln, bias=eps, reads=[var_r.r], writes=[var_r.r])
        env.act(rstd_r[:], var_r[:], AF.Exp, scale=-0.5,
                reads=[var_r.r], writes=[rstd_r.r])
        env.v("tensor_mul", mr_r[:], m_r[:], rstd_r[:],
              reads=[m_r.r, rstd_r.r], writes=[mr_r.r])
        ps2, ps3 = env.pf[2], env.pf[3]
        env.mm(ps2[:, :], env.ones_f32[0:1, :], rstd_r[:],
               [env.ones_f32.r, rstd_r.r], ps2.r, True, True)
        env.mm(ps3[:, :], env.ones_f32[0:1, :], mr_r[:],
               [env.ones_f32.r, mr_r.r], ps3.r, True, True)
        env.v("tensor_copy", rb[:], ps2[:, :], reads=[ps2.r], writes=[rb.r])
        for cc in range(8):
            tm = tmp[cc % 2]
            env.v("tensor_mul", tm[:], xb[par + cc][:], rb[:],
                  reads=[xb[par + cc].r, rb.r], writes=[tm.r])
            env.v("tensor_sub", tm[:], tm[:], ps3[:, :],
                  reads=[ps3.r], writes=[tm.r])
            o_ = ob[cc % 2]
            if gamma_col is not None:
                env.v("tensor_scalar", o_[:], tm[:], gamma_col[:, cc:cc + 1],
                      beta_col[:, cc:cc + 1], ALU.mult, ALU.add,
                      reads=[tm.r, gamma_col.r, beta_col.r], writes=[o_.r])
            else:
                env.v("tensor_copy", o_[:], tm[:], reads=[tm.r], writes=[o_.r])
            if add_dram is not None:
                a_ = ad[cc % 2]
                env.ld("sync", a_[:], add_dram[cc * 128:(cc + 1) * 128, cs],
                       reads=[add_dram.r], writes=[a_.r])
                env.v("tensor_add", o_[:], o_[:], a_[:], reads=[a_.r],
                      writes=[o_.r])
            if out_f32 is not None:
                env.ld("sync", out_f32[cc * 128:(cc + 1) * 128, cs], o_[:],
                       reads=[o_.r], wappend=[out_f32.r])
            if out_bf is not None:
                ofb = obf[cc % 2]
                env.v("tensor_copy", ofb[:], o_[:], reads=[o_.r], writes=[ofb.r])
                env.ld("sync", out_bf[cc * 128:(cc + 1) * 128, cs], ofb[:],
                       reads=[ofb.r], wappend=[out_bf.r])
    p.close_scope()



def emit_l1_attn(env, x_d, xo_d, w, ncores=8, core_ids=None, phase="all",
                 hand=None):
    p, T = env.p, env.T
    EPSK = 1e-6
    DN = 1.0 / math.sqrt(8.0)
    RATIO = 0.125
    TQ = 64 * (T // 128)
    xn_d = p.dram("l1a_xn", [1024, T], BF16)
    emit_ln_dram(env, x_d, xn_d)
    hand = hand or {}
    q_d = p.dram("l1a_q", [1024, T], BF16)
    k_d = p.dram("l1a_k", [1024, T], BF16)
    v_d = hand.get("v") or p.dram("l1a_v", [1024, T], BF16)
    qf_d = hand.get("qf") or p.dram("l1a_qf", [16, 128, TQ], BF16)
    kdc_d = hand.get("kdc") or p.dram("l1a_kdc", [16, 128, TQ], F32)
    attn_d = p.dram("l1a_attn", [1024, T], BF16)
    t1_d = p.dram("l1a_t1", [1024, T], F32)
    out1_d = p.dram("l1a_out1", [1024, T], F32)
    h_d = p.dram("l1a_h", [4096, T], BF16)
    t3_d = p.dram("l1a_t3", [1024, T], F32)
    ar_in = hand.get("lmax") or p.dram("l1a_arin", [1, 1], F32)
    ar_out = p.dram("l1a_arout", [1, 1], F32)

    if phase == "B":
        emit_l1_attn_B(env, locals())
        return
    p.open_scope()
    xt = load_fm(env, xn_d, "l1a_xt")
    ws = WStream(env, 3, 512, "l1a_ws")
    st = [p.sbuf("l1a_st", [128, 512], BF16) for _ in range(2)]
    cnt = [0]

    def plain_evac(dst_d):
        def evac(o, tt, ps):
            i = cnt[0] % 2
            cnt[0] += 1
            s = st[i]
            env.v("tensor_copy", s[:], ps[:, :], reads=[ps.r], writes=[s.r])
            env.ld("sync", dst_d[o * 128:(o + 1) * 128, tt * 512:(tt + 1) * 512],
                   s[:], reads=[s.r], wappend=[dst_d.r])
        return evac

    emit_linear_fm(env, xt, w["Wq"], 1024, ws, plain_evac(q_d))
    emit_linear_fm(env, xt, w["Wk"], 1024, ws, plain_evac(k_d))
    emit_linear_fm(env, xt, w["Wv"], 1024, ws, plain_evac(v_d))
    p.close_scope()

    p.open_scope()
    projT = p.sbuf("l1a_pT", [64, 64], BF16)
    env.ld("gpsimd", projT[:], w["projT"].ap, reads=[w["projT"].r], writes=[projT.r])
    qh = p.sbuf("l1a_qh", [64, T], BF16)
    kh = p.sbuf("l1a_kh", [64, T], BF16)
    sqh = p.sbuf("l1a_sqh", [64, T], BF16)
    dg_row = p.sbuf("l1a_dgr", [1, 512], F32)
    dd = p.sbuf("l1a_dd", [128, TQ], F32)
    diag = p.sbuf("l1a_diag", [128, T // 128], F32)
    mx = p.sbuf("l1a_mx", [128, 1], F32)
    bias_c = p.sbuf("l1a_bc", [128, 1], F32)
    kmax = p.sbuf("l1a_kmax", [128, 1], F32)
    ee = p.sbuf("l1a_ee", [128, 64], F32)
    qfb = p.sbuf("l1a_qfb", [128, TQ], BF16)
    kdcb = [p.sbuf("l1a_kdcb", [128, TQ], F32) for _ in range(1)]
    rowm = p.sbuf("l1a_rowm", [1, 128], F32)
    cmax = p.sbuf("l1a_cmax", [1, 1], F32)

    def feats(src_d, h, is_q):
        env.ld("sync", (qh if is_q else kh)[:], src_d[h * 64:(h + 1) * 64, :],
               reads=[src_d.r], writes=[(qh if is_q else kh).r])
        hh = qh if is_q else kh
        for qt in range(T // 128):
            ps = env.pf[qt % 2]
            env.mm(ps[0:128, 0:64], hh[:, qt * 128:(qt + 1) * 128], projT[:],
                   [hh.r, projT.r], ps.r, True, True)
            env.act(dd[:, qt * 64:(qt + 1) * 64], ps[0:128, 0:64], AF.Identity,
                    scale=DN, reads=[ps.r], writes=[dd.r])
        env.act(sqh[:], hh[:], AF.Square, reads=[hh.r], writes=[sqh.r])
        for tt in range(T // 512):
            ps = env.pf[2]
            env.mm(ps[0:1, :], env.ones_bf[0:64, 0:1], sqh[:, tt * 512:(tt + 1) * 512],
                   [env.ones_bf.r, sqh.r], ps.r, True, True)
            p.emit("scalar", lambda e, o=dg_row[:], i=ps[0:1, :]:
                   e.mul(o, i, 0.0625), reads=[ps.r], writes=[dg_row.r])
            for qb in range(4):
                ps2 = env.pf[3]
                env.transpose(ps2[0:128, 0:1], dg_row[0:1, qb * 128:(qb + 1) * 128],
                              env.eye_f32, [dg_row.r], ps2.r)
                env.v("tensor_copy", diag[:, tt * 4 + qb:tt * 4 + qb + 1],
                      ps2[0:128, 0:1], reads=[ps2.r], writes=[diag.r])
        return hh

    for h in range(16):
        feats(q_d, h, True)
        for qt in range(T // 128):
            blk = dd[:, qt * 64:(qt + 1) * 64]
            env.v("reduce_max", mx[:], blk, AX.X, reads=[dd.r], writes=[mx.r])
            env.v("tensor_add", bias_c[:], mx[:], diag[:, qt:qt + 1],
                  reads=[mx.r, diag.r], writes=[bias_c.r])
            env.v("tensor_scalar", bias_c[:], bias_c[:], -1.0, None, ALU.mult,
                  writes=[bias_c.r])
            env.act(ee[:], blk, AF.Exp, bias=bias_c[:, 0:1],
                    reads=[dd.r, bias_c.r], writes=[ee.r])
            env.v("tensor_scalar", qfb[:, qt * 64:(qt + 1) * 64], ee[:], EPSK,
                  RATIO, ALU.add, ALU.mult, reads=[ee.r], writes=[qfb.r])
        env.ld("sync", qf_d.ap[h], qfb[:], reads=[qfb.r], wappend=[qf_d.r])
        feats(k_d, h, False)
        kb = kdcb[0]
        for qt in range(T // 128):
            blk = dd[:, qt * 64:(qt + 1) * 64]
            env.v("reduce_max", mx[:], blk, AX.X, reads=[dd.r], writes=[mx.r])
            if h == 0 and qt == 0:
                env.v("tensor_copy", kmax[:], mx[:], reads=[mx.r], writes=[kmax.r])
            else:
                env.v("tensor_max", kmax[:], kmax[:], mx[:], reads=[mx.r],
                      writes=[kmax.r])
            env.v("tensor_scalar", kb[:, qt * 64:(qt + 1) * 64], blk,
                  diag[:, qt:qt + 1], None, ALU.subtract,
                  reads=[dd.r, diag.r], writes=[kb.r])
        env.ld("sync", kdc_d.ap[h], kb[:], reads=[kb.r], wappend=[kdc_d.r])
    psx = env.pf[4]
    env.transpose(psx[0:1, 0:128], kmax[:, 0:1], env.eye_f32, [kmax.r], psx.r)
    env.v("tensor_copy", rowm[:], psx[0:1, 0:128], reads=[psx.r], writes=[rowm.r])
    env.v("reduce_max", cmax[:], rowm[:], AX.X, reads=[rowm.r], writes=[cmax.r])
    env.ld("sync", ar_in.ap, cmax[:], reads=[cmax.r], writes=[ar_in.r])
    if phase == "A":
        p.close_scope()
        return
    groups = [list(range(ncores))] if core_ids is None else [core_ids]
    p.collective(lambda e, g=groups: e.collective_compute(
        "AllReduce", ALU.max, g, [ar_in.ap], [ar_out.ap]),
        reads=[ar_in.r], writes=[ar_out.r])
    p.close_scope()
    emit_l1_attn_B(env, locals())


def emit_l1_attn_B(env, L):
    p, T = env.p, env.T
    (x_d, xo_d, w, xn_d, q_d, k_d, v_d, qf_d, kdc_d, attn_d, t1_d, out1_d,
     h_d, t3_d, ar_out, phase, hand, EPSK, DN, RATIO, TQ) = (
        L["x_d"], L["xo_d"], L["w"], L["xn_d"], L["q_d"], L["k_d"], L["v_d"],
        L["qf_d"], L["kdc_d"], L["attn_d"], L["t1_d"], L["out1_d"], L["h_d"],
        L["t3_d"], L["ar_out"], L["phase"], L["hand"], L["EPSK"], L["DN"],
        L["RATIO"], L["TQ"])

    p.open_scope()
    mxs = p.sbuf("l1a_mxs", [1, 1], F32)
    nmxb = p.sbuf("l1a_nmxb", [128, 1], F32)
    if phase == "B":
        m8 = p.sbuf("l1a_m8", [8, 1], F32)
        m8r = p.sbuf("l1a_m8r", [1, 8], F32)
        env.ld("sync", m8[:], hand["maxes8"].ap, reads=[hand["maxes8"].r],
               writes=[m8.r])
        pst = env.pf[4]
        env.transpose(pst[0:1, 0:8], m8[0:8, 0:1], env.eye_f32, [m8.r], pst.r)
        env.v("tensor_copy", m8r[:], pst[0:1, 0:8], reads=[pst.r], writes=[m8r.r])
        env.v("reduce_max", mxs[:], m8r[:], AX.X, reads=[m8r.r], writes=[mxs.r])
    else:
        env.ld("sync", mxs[:], ar_out.ap, reads=[ar_out.r], writes=[mxs.r])
    psb = env.pf[5]
    env.mm(psb[0:128, 0:1], env.ones_f32[0:1, :], mxs[:], [env.ones_f32.r, mxs.r],
           psb.r, True, True)
    env.p.emit("scalar", lambda e, o=nmxb[:], i=psb[0:128, 0:1]: e.mul(o, i, -1.0),
               reads=[psb.r], writes=[nmxb.r])
    vh = p.sbuf("l1a_vh", [64, T], BF16)
    vtok = [p.sbuf("l1a_vt", [128, 65], BF16) for _ in range(T // 128)]
    for kt in range(T // 128):
        env.v("memset", vtok[kt][:, 64:65], 1.0, writes=[vtok[kt].r])
    kdcb2 = p.sbuf("l1a_kdcb2", [128, TQ], F32)
    kfb = p.sbuf("l1a_kfb", [128, TQ], BF16)
    qfb2 = p.sbuf("l1a_qfb2", [128, TQ], BF16)
    qffm = p.sbuf("l1a_qffm", [64, T], BF16)
    kv65 = p.sbuf("l1a_kv65", [64, 65], BF16)
    ee2 = p.sbuf("l1a_ee2", [128, 64], F32)
    rden = p.sbuf("l1a_rden", [1, 512], F32)
    rbv = p.sbuf("l1a_rbv", [64, 512], F32)
    atn = p.sbuf("l1a_atn", [64, 512], BF16)
    for h in range(16):
        env.ld("sync", vh[:], v_d[h * 64:(h + 1) * 64, :], reads=[v_d.r],
               writes=[vh.r])
        for g in range(T // 512):
            bank = env.pb[g % 2]
            for j in range(4):
                kt = g * 4 + j
                env.transpose(bank[0:128, j * 256:j * 256 + 64],
                              vh[:, kt * 128:(kt + 1) * 128], env.eye_bf,
                              [vh.r], bank.r)
            for j in range(4):
                kt = g * 4 + j
                env.v("tensor_copy", vtok[kt][:, 0:64],
                      bank[0:128, j * 256:j * 256 + 64],
                      reads=[bank.r], writes=[vtok[kt].r])
        env.ld("sync", kdcb2[:], kdc_d.ap[h], reads=[kdc_d.r], writes=[kdcb2.r])
        env.ld("sync", qfb2[:], qf_d.ap[h], reads=[qf_d.r], writes=[qfb2.r])
        for qt in range(T // 128):
            env.act(ee2[:], kdcb2[:, qt * 64:(qt + 1) * 64], AF.Exp,
                    bias=nmxb[:, 0:1], reads=[kdcb2.r, nmxb.r], writes=[ee2.r])
            env.v("tensor_scalar", kfb[:, qt * 64:(qt + 1) * 64], ee2[:], EPSK,
                  RATIO, ALU.add, ALU.mult, reads=[ee2.r], writes=[kfb.r])
        for qt in range(T // 128):
            psq = env.pb[qt % 2]
            env.transpose(psq[0:64, 0:128], qfb2[:, qt * 64:(qt + 1) * 64],
                          env.eye_bf, [qfb2.r], psq.r)
            env.v("tensor_copy", qffm[:, qt * 128:(qt + 1) * 128],
                  psq[0:64, 0:128], reads=[psq.r], writes=[qffm.r])
        pskv = env.pf[0]
        for kt in range(T // 128):
            env.mm(pskv[0:64, 0:65], kfb[:, kt * 64:(kt + 1) * 64],
                   vtok[kt][:, 0:65], [kfb.r, vtok[kt].r], pskv.r,
                   kt == 0, kt == T // 128 - 1)
        env.v("tensor_copy", kv65[:], pskv[0:64, 0:65], reads=[pskv.r],
              writes=[kv65.r])
        for tt in range(T // 512):
            psn = env.pf[1 + (tt % 2)]
            env.mm(psn[0:65, :], kv65[:], qffm[:, tt * 512:(tt + 1) * 512],
                   [kv65.r, qffm.r], psn.r, True, True)
            env.v("reciprocal", rden[:], psn[64:65, :], reads=[psn.r],
                  writes=[rden.r])
            ps4 = env.pf[4]
            env.mm(ps4[0:64, :], env.ones_f32[0:1, 0:64], rden[:],
                   [env.ones_f32.r, rden.r], ps4.r, True, True)
            env.v("tensor_copy", rbv[:], ps4[0:64, :], reads=[ps4.r], writes=[rbv.r])
            env.v("tensor_mul", atn[:], psn[0:64, :], rbv[:],
                  reads=[psn.r, rbv.r], writes=[atn.r])
            env.ld("sync", attn_d[h * 64:(h + 1) * 64, tt * 512:(tt + 1) * 512],
                   atn[:], reads=[atn.r], wappend=[attn_d.r])
    p.close_scope()

    p.open_scope()
    at = load_fm(env, attn_d, "l1a_at")
    ws2 = WStream(env, 3, 512, "l1a_ws2")
    xnb = [p.sbuf("l1a_xnb", [128, 512], BF16) for _ in range(2)]
    t1b = [p.sbuf("l1a_t1b", [128, 512], F32) for _ in range(2)]
    cnt2 = [0]

    def t1_evac(o, tt, ps):
        i = cnt2[0] % 2
        cnt2[0] += 1
        cs = slice(tt * 512, (tt + 1) * 512)
        env.ld("sync", xnb[i][:], xn_d[o * 128:(o + 1) * 128, cs],
               reads=[xn_d.r], writes=[xnb[i].r])
        env.v("tensor_add", t1b[i][:], ps[:, :], xnb[i][:],
              reads=[ps.r, xnb[i].r], writes=[t1b[i].r])
        env.ld("sync", t1_d[o * 128:(o + 1) * 128, cs], t1b[i][:],
               reads=[t1b[i].r], wappend=[t1_d.r])

    emit_linear_fm(env, at, w["Wo"], 1024, ws2, t1_evac)
    p.close_scope()
    p.open_scope()
    lcols = {}
    for n in ("ln1g", "ln1b", "ln2g", "ln2b", "b2"):
        lcols[n] = p.sbuf("l1a_" + n, [128, 8], F32)
        env.ld("sync", lcols[n][:], w[n].ap, reads=[w[n].r], writes=[lcols[n].r])
    b1c = p.sbuf("l1a_b1c", [128, 32], F32)
    env.ld("sync", b1c[:], w["b1"].ap, reads=[w["b1"].r], writes=[b1c.r])
    emit_lnx(env, t1_d, 1e-6, gamma_col=lcols["ln1g"], beta_col=lcols["ln1b"],
             out_f32=out1_d)

    p.open_scope()
    o1t = [p.sbuf("l1a_o1t", [128, T], BF16) for _ in range(8)]
    for c in range(8):
        env.ld("gpsimd", o1t[c][:], out1_d[c * 128:(c + 1) * 128, :],
               reads=[out1_d.r], writes=[o1t[c].r])
    ws3 = WStream(env, 3, 512, "l1a_ws3")
    zn = [p.sbuf("l1a_zn", [128, 512], F32) for _ in range(2)]
    zp = [p.sbuf("l1a_zp", [128, 512], F32) for _ in range(2)]
    eb = [p.sbuf("l1a_eb", [128, 512], F32) for _ in range(2)]
    hb = [p.sbuf("l1a_hb", [128, 512], BF16) for _ in range(2)]
    cnt3 = [0]

    def elu_evac(f, tt, ps):
        i = cnt3[0] % 2
        cnt3[0] += 1
        env.v("tensor_scalar", zn[i][:], ps[:, :], b1c[:, f:f + 1], 0.0,
              ALU.add, ALU.min, reads=[ps.r, b1c.r], writes=[zn[i].r])
        env.v("tensor_scalar", zp[i][:], ps[:, :], b1c[:, f:f + 1], 0.0,
              ALU.add, ALU.max, reads=[ps.r, b1c.r], writes=[zp[i].r])
        env.act(eb[i][:], zn[i][:], AF.Exp, reads=[zn[i].r], writes=[eb[i].r])
        env.v("tensor_scalar", eb[i][:], eb[i][:], -1.0, None, ALU.add,
              writes=[eb[i].r])
        env.v("tensor_add", hb[i][:], eb[i][:], zp[i][:],
              reads=[eb[i].r, zp[i].r], writes=[hb[i].r])
        env.ld("sync", h_d[f * 128:(f + 1) * 128, tt * 512:(tt + 1) * 512],
               hb[i][:], reads=[hb[i].r], wappend=[h_d.r])

    emit_linear_fm(env, o1t, w["W1"], 4096, ws3, elu_evac)
    p.close_scope()

    TM = min(1024, T)
    p.open_scope()
    ws4 = WStream(env, 6, 512, "l1a_ws4")
    ht = [p.sbuf("l1a_ht", [128, TM], BF16) for _ in range(32)]
    o1b = [p.sbuf("l1a_o1b", [128, 512], F32) for _ in range(2)]
    t3b = [p.sbuf("l1a_t3b", [128, 512], F32) for _ in range(2)]
    cnt4 = [0]
    tmref = [0]

    def ffn_evac(o, tt, ps):
        i = cnt4[0] % 2
        cnt4[0] += 1
        c0 = tmref[0] * TM
        cs = slice(c0 + tt * 512, c0 + (tt + 1) * 512)
        env.ld("sync", o1b[i][:], out1_d[o * 128:(o + 1) * 128, cs],
               reads=[out1_d.r], writes=[o1b[i].r])
        env.v("tensor_scalar", t3b[i][:], ps[:, :], lcols["b2"][:, o:o + 1],
              None, ALU.add, reads=[ps.r, lcols["b2"].r], writes=[t3b[i].r])
        env.v("tensor_add", t3b[i][:], t3b[i][:], o1b[i][:],
              reads=[o1b[i].r], writes=[t3b[i].r])
        env.ld("sync", t3_d[o * 128:(o + 1) * 128, cs], t3b[i][:],
               reads=[t3b[i].r], wappend=[t3_d.r])

    for tm in range(T // TM):
        tmref[0] = tm
        c0 = tm * TM
        for k in range(32):
            env.ld("sync", ht[k][:], h_d[k * 128:(k + 1) * 128, c0:c0 + TM],
                   reads=[h_d.r], writes=[ht[k].r])
        emit_linear_fm(env, ht, w["W2"], 1024, ws4, ffn_evac, kn=32,
                       col0=0, ncols=TM)
    p.close_scope()
    emit_lnx(env, t3_d, 1e-6, gamma_col=lcols["ln2g"], beta_col=lcols["ln2b"],
             add_dram=x_d, out_f32=xo_d)
    p.close_scope()


NS_QUINTIC = 13
NS_CUBIC = 5


def emit_l2_polar(env, wraw_d, wpol_d):
    p = env.p
    p.open_scope()
    mk = lambda nm, dt=BF16: [p.sbuf(nm, [128, 1024], dt) for _ in range(8)]
    Xh, Xl = mk("ns_xh"), mk("ns_xl")
    Yh, Yl = mk("ns_yh"), mk("ns_yl")
    Gh, Gl = mk("ns_gh"), mk("ns_gl")
    Mh, Ml = mk("ns_mh"), mk("ns_ml")
    Hf = mk("ns_hf", F32)
    wt = [p.sbuf("ns_wt", [128, 512], F32) for _ in range(2)]
    sqt16 = [p.sbuf("ns_sq", [128, 512], BF16) for _ in range(16)]
    srow = p.sbuf("ns_srow", [1, 1], F32)
    rn = p.sbuf("ns_rn", [1, 1], F32)
    rnc = p.sbuf("ns_rnc", [128, 1], F32)
    tf = [p.sbuf("ns_tf", [128, 512], F32) for _ in range(2)]

    def split(ps_view, ps_r, hi_view, hi_r, lo_view, lo_r):
        env.v("tensor_copy", hi_view, ps_view, reads=[ps_r], writes=[hi_r])
        env.v("tensor_sub", lo_view, ps_view, hi_view, reads=[ps_r, hi_r],
              writes=[lo_r])

    ps = env.pf[4]
    n = 0
    for r in range(8):
        for half in range(2):
            w_ = wt[n % 2]
            env.ld("sync", w_[:], wraw_d[r * 128:(r + 1) * 128,
                                         half * 512:(half + 1) * 512],
                   reads=[wraw_d.r], writes=[w_.r])
            env.act(sqt16[n][:], w_[:], AF.Square, reads=[w_.r], writes=[sqt16[n].r])
            env.mm(ps[0:1, :], env.ones_bf[:, 0:1], sqt16[n][:],
                   [env.ones_bf.r, sqt16[n].r], ps.r, n == 0, n == 15)
            n += 1
    env.v("reduce_sum", srow[:], ps[0:1, :], AX.X, reads=[ps.r], writes=[srow.r])
    env.act(srow[:], srow[:], AF.Ln, reads=[srow.r], writes=[srow.r])
    env.act(rn[:], srow[:], AF.Exp, scale=-0.5, reads=[srow.r], writes=[rn.r])
    ps5 = env.pf[5]
    env.mm(ps5[0:128, 0:1], env.ones_f32[0:1, :], rn[:], [env.ones_f32.r, rn.r],
           ps5.r, True, True)
    env.v("tensor_copy", rnc[:], ps5[0:128, 0:1], reads=[ps5.r], writes=[rnc.r])
    for r in range(8):
        for half in range(2):
            hs = slice(half * 512, (half + 1) * 512)
            w_ = wt[(r * 2 + half) % 2]
            env.ld("sync", w_[:], wraw_d[r * 128:(r + 1) * 128, hs],
                   reads=[wraw_d.r], writes=[w_.r])
            t_ = tf[(r * 2 + half) % 2]
            env.v("tensor_scalar", t_[:], w_[:], rnc[:, 0:1], None, ALU.mult,
                  reads=[w_.r, rnc.r], writes=[t_.r])
            env.v("tensor_copy", Xh[r][0:128, hs], t_[:], reads=[t_.r],
                  writes=[Xh[r].r])
            env.v("tensor_sub", Xl[r][0:128, hs], t_[:], Xh[r][0:128, hs],
                  reads=[t_.r, Xh[r].r], writes=[Xl[r].r])

    def transpose_to(srcs, dsts):
        for ci in range(8):
            for rj in range(8):
                bank = env.pb[rj % 2]
                env.transpose(bank[0:128, 0:128],
                              srcs[rj][0:128, ci * 128:(ci + 1) * 128],
                              env.eye_bf, [srcs[rj].r], bank.r)
                env.v("tensor_copy", dsts[ci][0:128, rj * 128:(rj + 1) * 128],
                      bank[0:128, 0:128], reads=[bank.r], writes=[dsts[ci].r])

    transpose_to(Xh, Yh)
    transpose_to(Xl, Yl)

    coeffs = [(3.4445, -4.7750, 2.0315)] * NS_QUINTIC + [(1.5, -0.5, 0.0)] * NS_CUBIC
    for it, (ca, cb, cc_) in enumerate(coeffs):
        quintic = cc_ != 0.0
        for ci in range(8):
            for half in range(2):
                hs = slice(half * 512, (half + 1) * 512)
                psg = env.pf[(ci * 2 + half) % 2]
                combos = [(Xh, Xh), (Xh, Xl), (Xl, Xh)]
                nmm = 0
                for (A, B) in combos:
                    for r in range(8):
                        env.mm(psg[:, :], A[r][0:128, ci * 128:(ci + 1) * 128],
                               B[r][0:128, hs], [A[r].r, B[r].r], psg.r,
                               nmm == 0, nmm == 23)
                        nmm += 1
                split(psg[:, :], psg.r, Gh[ci][0:128, hs], Gh[ci].r,
                      Gl[ci][0:128, hs], Gl[ci].r)
        if quintic:
            for ci in range(8):
                for half in range(2):
                    hs = slice(half * 512, (half + 1) * 512)
                    psh = env.pf[2 + (ci * 2 + half) % 2]
                    combos = [(Gh, Gh), (Gh, Gl), (Gl, Gh)]
                    nmm = 0
                    for (A, B) in combos:
                        for r in range(8):
                            env.mm(psh[:, :], A[r][0:128, ci * 128:(ci + 1) * 128],
                                   B[r][0:128, hs], [A[r].r, B[r].r], psh.r,
                                   nmm == 0, nmm == 23)
                            nmm += 1
                    env.v("tensor_copy", Hf[ci][0:128, hs], psh[:, :],
                          reads=[psh.r], writes=[Hf[ci].r])
        for ci in range(8):
            for half in range(2):
                hs = slice(half * 512, (half + 1) * 512)
                t_ = tf[0]
                env.v("tensor_add", t_[:], Gh[ci][0:128, hs], Gl[ci][0:128, hs],
                      reads=[Gh[ci].r, Gl[ci].r], writes=[t_.r])
                if quintic:
                    t2 = tf[1]
                    env.v("tensor_scalar", t2[:], Hf[ci][0:128, hs], cc_, None,
                          ALU.mult, reads=[Hf[ci].r], writes=[t2.r])
                    env.v("scalar_tensor_tensor", t_[:], t_[:], cb, t2[:],
                          ALU.mult, ALU.add, reads=[t2.r], writes=[t_.r])
                else:
                    env.v("tensor_scalar", t_[:], t_[:], cb, None, ALU.mult,
                          writes=[t_.r])
                dlo = half * 512
                if dlo <= ci * 128 < dlo + 512:
                    off = ci * 128 - dlo
                    env.v("scalar_tensor_tensor", t_[:, off:off + 128],
                          env.eye_f32[:], ca, t_[:, off:off + 128],
                          ALU.mult, ALU.add, reads=[env.eye_f32.r], writes=[t_.r])
                env.v("tensor_copy", Mh[ci][0:128, hs], t_[:], reads=[t_.r],
                      writes=[Mh[ci].r])
                env.v("tensor_sub", Ml[ci][0:128, hs], t_[:], Mh[ci][0:128, hs],
                      reads=[t_.r, Mh[ci].r], writes=[Ml[ci].r])
        for rj in range(8):
            for half in range(2):
                hs = slice(half * 512, (half + 1) * 512)
                psx = env.pf[(rj * 2 + half) % 2]
                combos = [(Yh, Mh), (Yh, Ml), (Yl, Mh)]
                nmm = 0
                for (A, B) in combos:
                    for k in range(8):
                        env.mm(psx[:, :], A[k][0:128, rj * 128:(rj + 1) * 128],
                               B[k][0:128, hs], [A[k].r, B[k].r], psx.r,
                               nmm == 0, nmm == 23)
                        nmm += 1
                split(psx[:, :], psx.r, Gh[rj][0:128, hs], Gh[rj].r,
                      Gl[rj][0:128, hs], Gl[rj].r)
        Xh, Gh = Gh, Xh
        Xl, Gl = Gl, Xl
        transpose_to(Xh, Yh)
        transpose_to(Xl, Yl)
    for r in range(8):
        env.ld("sync", wpol_d[r * 128:(r + 1) * 128, :], Xh[r][:],
               reads=[Xh[r].r], wappend=[wpol_d.r])
    p.close_scope()


def emit_l2_attn(env, x_d, xo_d, w):
    p, T = env.p, env.T
    LANES, HD2 = 4, 256
    wpol_d = p.dram("l2a_wpol", [1024, 1024], BF16)
    emit_l2_polar(env, w["wraw"], wpol_d)

    xn_d = p.dram("l2a_xn", [1024, T], BF16)
    emit_ln_dram(env, x_d, xn_d)
    qA_d = p.dram("l2a_qA", [T, 1024], F32)
    qB_d = p.dram("l2a_qB", [T, 1024], F32)
    v_d = p.dram("l2a_v", [T, 1024], BF16)

    p.open_scope()
    xt = load_fm(env, xn_d, "l2a_xt")
    wu = [p.sbuf("l2a_wu", [128, 1024], BF16) for _ in range(8)]
    wv = [p.sbuf("l2a_wv", [128, 1024], BF16) for _ in range(8)]
    for k in range(8):
        env.ld("gpsimd", wu[k][:], w["W_upT"][k * 128:(k + 1) * 128, :],
               reads=[w["W_upT"].r], writes=[wu[k].r])
        env.ld("gpsimd", wv[k][:], w["W_vT"][k * 128:(k + 1) * 128, :],
               reads=[w["W_vT"].r], writes=[wv[k].r])
    ob = [p.sbuf("l2a_ob", [128, 512], F32) for _ in range(2)]
    obb = [p.sbuf("l2a_obb", [128, 512], BF16) for _ in range(2)]
    for qt in range(T // 128):
        for oh in range(2):
            ps = env.pf[oh]
            for k in range(8):
                env.mm(ps[:, :], xt[k][0:128, qt * 128:(qt + 1) * 128],
                       wu[k][:, oh * 512:(oh + 1) * 512], [xt[k].r, wu[k].r],
                       ps.r, k == 0, k == 7)
            o_ = ob[oh]
            env.v("tensor_copy", o_[:], ps[:, :], reads=[ps.r], writes=[o_.r])
            env.ld("sync", qA_d[qt * 128:(qt + 1) * 128, oh * 512:(oh + 1) * 512],
                   o_[:], reads=[o_.r], wappend=[qA_d.r])
            ps2 = env.pf[2 + oh]
            for k in range(8):
                env.mm(ps2[:, :], xt[k][0:128, qt * 128:(qt + 1) * 128],
                       wv[k][:, oh * 512:(oh + 1) * 512], [xt[k].r, wv[k].r],
                       ps2.r, k == 0, k == 7)
            o2 = obb[oh]
            env.v("tensor_copy", o2[:], ps2[:, :], reads=[ps2.r], writes=[o2.r])
            env.ld("sync", v_d[qt * 128:(qt + 1) * 128, oh * 512:(oh + 1) * 512],
                   o2[:], reads=[o2.r], wappend=[v_d.r])
    p.close_scope()

    p.open_scope()
    wp = [p.sbuf("l2a_wp", [128, 1024], BF16) for _ in range(8)]
    for k in range(8):
        env.ld("sync", wp[k][:], wpol_d[k * 128:(k + 1) * 128, :],
               reads=[wpol_d.r], writes=[wp[k].r])
    idt = p.sbuf("l2a_idt", [128, 1024], F32)
    idr = p.sbuf("l2a_idr", [1, 1024], F32)
    env.ld("sync", idr[:], w["id_row"].ap, reads=[w["id_row"].r], writes=[idr.r])
    for oh in range(2):
        psb = env.pf[4]
        env.mm(psb[:, :], env.ones_f32[0:1, :], idr[0:1, oh * 512:(oh + 1) * 512],
               [env.ones_f32.r, idr.r], psb.r, True, True)
        env.v("tensor_copy", idt[:, oh * 512:(oh + 1) * 512], psb[:, :],
              reads=[psb.r], writes=[idt.r])
    lt = [p.sbuf("l2a_lt", [128, 1024], F32) for _ in range(2)]
    ct = [p.sbuf("l2a_ct", [128, 1024], F32) for _ in range(2)]
    sb = p.sbuf("l2a_sb", [128, 1024], BF16)
    sfm = p.sbuf("l2a_sfm", [128, 1024], BF16)
    vsb = p.sbuf("l2a_vsb", [128, 1024], F32)
    prod = p.sbuf("l2a_prod", [128, 256], F32)
    sc = p.sbuf("l2a_sc", [128, 16], F32)
    att = p.sbuf("l2a_att", [128, 16], F32)
    rmx = p.sbuf("l2a_rmx", [128, 1], F32)
    zb = p.sbuf("l2a_zb", [128, 1024], F32)
    ms4 = p.sbuf("l2a_ms4", [128, 4], F32)
    rs4 = p.sbuf("l2a_rs4", [128, 4], F32)
    newt = [p.sbuf("l2a_new", [128, 1024], F32) for _ in range(2)]
    nsteps = int(math.log2(T))
    qd = [qA_d, qB_d]
    for step in range(nsteps):
        d = 1 << step
        src, dst = qd[step % 2], qd[(step + 1) % 2]
        for qt in range(T // 128):
            ii = qt % 2
            l_, c_ = lt[ii], ct[ii]
            g0 = qt * 128
            env.ld("sync", c_[:], src[g0:g0 + 128, :], reads=[src.r], writes=[c_.r])
            if g0 >= d:
                env.ld("sync", l_[:], src[g0 - d:g0 - d + 128, :],
                       reads=[src.r], writes=[l_.r])
            elif g0 + 128 <= d:
                env.v("tensor_copy", l_[:], idt[:], reads=[idt.r], writes=[l_.r])
            else:
                nid = d - g0
                env.v("tensor_copy", l_[0:nid, :], idt[0:nid, :],
                      reads=[idt.r], writes=[l_.r])
                env.ld("sync", l_[nid:128, :], src[0:128 - nid, :],
                       reads=[src.r], writes=[l_.r])
            env.v("tensor_add", sb[:], l_[:], c_[:], reads=[l_.r, c_.r],
                  writes=[sb.r])
            bank = env.pb[qt % 2]
            for cc in range(8):
                env.transpose(bank[0:128, cc * 128:(cc + 1) * 128],
                              sb[:, cc * 128:(cc + 1) * 128], env.eye_bf,
                              [sb.r], bank.r)
            env.v("tensor_copy", sfm[:], bank[0:128, 0:1024], reads=[bank.r],
                  writes=[sfm.r])
            for oh in range(2):
                psv = env.pf[oh]
                for k in range(8):
                    env.mm(psv[:, :], sfm[:, k * 128:(k + 1) * 128],
                           wp[k][:, oh * 512:(oh + 1) * 512], [sfm.r, wp[k].r],
                           psv.r, k == 0, k == 7)
                env.v("tensor_copy", vsb[:, oh * 512:(oh + 1) * 512], psv[:, :],
                      reads=[psv.r], writes=[vsb.r])
            for l in range(LANES):
                for m in range(LANES):
                    env.v("tensor_mul", prod[:], l_[:, l * HD2:(l + 1) * HD2],
                          c_[:, m * HD2:(m + 1) * HD2], reads=[l_.r, c_.r],
                          writes=[prod.r])
                    env.v("reduce_sum", sc[:, l * 4 + m:l * 4 + m + 1], prod[:],
                          AX.X, reads=[prod.r], writes=[sc.r])
            for l in range(LANES):
                blk = sc[:, l * 4:(l + 1) * 4]
                env.v("reduce_max", rmx[:], blk, AX.X, reads=[sc.r], writes=[rmx.r])
                env.v("tensor_scalar", rmx[:], rmx[:], -0.0625, None, ALU.mult,
                      writes=[rmx.r])
                env.act(att[:, l * 4:(l + 1) * 4], blk, AF.Exp, scale=0.0625,
                        bias=rmx[:, 0:1], reads=[sc.r, rmx.r], writes=[att.r])
                env.v("reduce_sum", rmx[:], att[:, l * 4:(l + 1) * 4], AX.X,
                      reads=[att.r], writes=[rmx.r])
                env.v("reciprocal", rmx[:], rmx[:], writes=[rmx.r])
                env.v("tensor_scalar", att[:, l * 4:(l + 1) * 4],
                      att[:, l * 4:(l + 1) * 4], rmx[:, 0:1], None, ALU.mult,
                      reads=[rmx.r], writes=[att.r])
            for l in range(LANES):
                for m in range(LANES):
                    zsl = zb[:, l * HD2:(l + 1) * HD2]
                    if m == 0:
                        env.v("tensor_scalar", zsl, vsb[:, 0:HD2],
                              att[:, l * 4:l * 4 + 1], None, ALU.mult,
                              reads=[vsb.r, att.r], writes=[zb.r])
                    else:
                        env.v("scalar_tensor_tensor", zsl,
                              vsb[:, m * HD2:(m + 1) * HD2],
                              att[:, l * 4 + m:l * 4 + m + 1], zsl,
                              ALU.mult, ALU.add, reads=[vsb.r, att.r],
                              writes=[zb.r])
            for l in range(LANES):
                env.v("tensor_mul", prod[:], zb[:, l * HD2:(l + 1) * HD2],
                      zb[:, l * HD2:(l + 1) * HD2], reads=[zb.r], writes=[prod.r])
                env.v("reduce_sum", ms4[:, l:l + 1], prod[:], AX.X,
                      reads=[prod.r], writes=[ms4.r])
            env.act(ms4[:], ms4[:], AF.Ln, bias=1e-8, scale=1.0 / HD2,
                    reads=[ms4.r], writes=[ms4.r])
            env.act(rs4[:], ms4[:], AF.Exp, scale=-0.5, reads=[ms4.r],
                    writes=[rs4.r])
            nw = newt[ii]
            for l in range(LANES):
                env.v("tensor_scalar", rs4[:, l:l + 1], rs4[:, l:l + 1],
                      1.0 / (l + 1.0), None, ALU.mult, writes=[rs4.r])
                env.v("scalar_tensor_tensor", nw[:, l * HD2:(l + 1) * HD2],
                      zb[:, l * HD2:(l + 1) * HD2], rs4[:, l:l + 1],
                      l_[:, l * HD2:(l + 1) * HD2], ALU.mult, ALU.add,
                      reads=[zb.r, rs4.r, l_.r], writes=[nw.r])
            env.ld("sync", dst[g0:g0 + 128, :], nw[:], reads=[nw.r],
                   wappend=[dst.r])
    p.close_scope()

    qfin = qd[nsteps % 2]
    p.open_scope()
    pfm = [p.sbuf("l2a_pfm", [128, T], BF16) for _ in range(8)]
    qt_ = [p.sbuf("l2a_qt", [128, 1024], F32) for _ in range(2)]
    vt_ = [p.sbuf("l2a_vt", [128, 1024], BF16) for _ in range(2)]
    pb_ = p.sbuf("l2a_pb", [128, 1024], BF16)
    for qt in range(T // 128):
        ii = qt % 2
        env.ld("sync", qt_[ii][:], qfin[qt * 128:(qt + 1) * 128, :],
               reads=[qfin.r], writes=[qt_[ii].r])
        env.ld("sync", vt_[ii][:], v_d[qt * 128:(qt + 1) * 128, :],
               reads=[v_d.r], writes=[vt_[ii].r])
        env.v("tensor_mul", pb_[:], qt_[ii][:], vt_[ii][:],
              reads=[qt_[ii].r, vt_[ii].r], writes=[pb_.r])
        bank = env.pb[qt % 2]
        for cc in range(8):
            env.transpose(bank[0:128, cc * 128:(cc + 1) * 128],
                          pb_[:, cc * 128:(cc + 1) * 128], env.eye_bf,
                          [pb_.r], bank.r)
        for cc in range(8):
            env.v("tensor_copy", pfm[cc][0:128, qt * 128:(qt + 1) * 128],
                  bank[0:128, cc * 128:(cc + 1) * 128], reads=[bank.r],
                  writes=[pfm[cc].r])
    ws = WStream(env, 3, 512, "l2a_ws")
    xcr = [p.sbuf("l2a_xc", [128, 512], F32) for _ in range(2)]
    rr = [p.sbuf("l2a_rr", [128, 512], F32) for _ in range(2)]
    cnt = [0]

    def out_evac(o, tt, ps):
        i = cnt[0] % 2
        cnt[0] += 1
        cs = slice(tt * 512, (tt + 1) * 512)
        env.ld("sync", xcr[i][:], x_d[o * 128:(o + 1) * 128, cs],
               reads=[x_d.r], writes=[xcr[i].r])
        env.v("tensor_add", rr[i][:], ps[:, :], xcr[i][:],
              reads=[ps.r, xcr[i].r], writes=[rr[i].r])
        env.ld("sync", xo_d[o * 128:(o + 1) * 128, cs], rr[i][:],
               reads=[rr[i].r], wappend=[xo_d.r])

    emit_linear_fm(env, pfm, w["W_projT"], 1024, ws, out_evac)
    p.close_scope()


def rope_tables(T):
    inv = 1.0 / 10000.0 ** (np.arange(32, dtype=np.float64) / 32.0)
    ang = np.arange(T, dtype=np.float64)[None, :] * inv[:, None]
    c32, s32 = np.cos(ang), np.sin(ang)
    cos = np.concatenate([c32] * 4, 0).astype(np.float32)
    sin = np.concatenate([s32] * 4, 0).astype(np.float32)
    return cos, sin


def build_module(T=4096, ncores=8, part=0):
    nc = bass.Bass("TRN2", target_bir_lowering=False)
    p = Prog(nc)
    env = Env(p, T)
    din = {}

    def D(name, shape):
        din[name] = p.dram_in(name, list(shape), F32)
        return din[name]

    x0 = D("x0", [1024, T]) if part in (0, 1) else None
    xout = p.dram_out("xout", [1024, T], F32) if part in (0, 2) else None
    xA = p.dram("xA", [1024, T], F32)
    xB = p.dram("xB", [1024, T], F32)

    p1 = part in (0, 1)
    p2 = part in (0, 2)
    w0a = {}
    if p1:
        w0a = {n: D("l0a_" + n, [1024, 1024]) for n in ("Wq", "Wk", "Wv", "Wo")}
        for b in ("bq", "bk", "bv", "bo"):
            w0a[b] = D("l0a_" + b, [128, 8])
        w0a["cos"] = D("l0a_cos", [128, T])
        w0a["sin"] = D("l0a_sin", [128, T])

    w0f = {}
    w0f_shapes = (("n1", (128, 8)),) if False else None
    for n, sh in () if not p1 else (("n1", (128, 8)), ("n2", (128, 8)), ("qn", (128, 1)),
                  ("kn", (128, 1)), ("wqT", (1024, 1024)), ("wkT", (1024, 1024)),
                  ("wvT", (1024, 1024)), ("woT", (1024, 1024)),
                  ("gateT", (1024, 4)), ("e0_upT", (1024, 2048)),
                  ("e0_downT", (2048, 1024)), ("e1_w1T", (1024, 2048)),
                  ("e1_w2T", (1024, 2048)), ("e1_downT", (2048, 1024)),
                  ("e2_l1T", (1024, 2048)), ("e2_l2T", (2048, 2048)),
                  ("e2_l3T", (2048, 2048)), ("e2_l4T", (2048, 1024)),
                  ("e3_downT", (1024, 1024)), ("e3_upT", (1024, 2048)),
                  ("e3_outT", (2048, 1024))):
        w0f[n] = D("l0f_" + n, sh)

    w1a = {}
    if p1:
        for n in ("Wq", "Wk", "Wv"):
            w1a[n] = D("l1a_" + n, (1024, 1024))
        w1a["projT"] = D("l1a_projT", (64, 64))
    if p2:
        w1a["Wo"] = D("l1a_Wo", (1024, 1024))
        w1a["W1"] = D("l1a_W1", (1024, 4096))
        w1a["W2"] = D("l1a_W2", (4096, 1024))
        w1a["b1"] = D("l1a_b1", (128, 32))
        for n in ("b2", "ln1g", "ln1b", "ln2g", "ln2b"):
            w1a[n] = D("l1a_" + n, (128, 8))

    w1f = {"routerT": D("l1f_routerT", (1024, 8)), "ebias": D("l1f_ebias", (8, 1)),
           "w1T": D("l1f_w1T", (8, 1024, 2048)), "w3T": D("l1f_w3T", (8, 1024, 2048)),
           "w2T": D("l1f_w2T", (8, 2048, 1024)), "sw1T": D("l1f_sw1T", (2, 1024, 2048)),
           "sw3T": D("l1f_sw3T", (2, 1024, 2048)), "sw2T": D("l1f_sw2T", (2, 2048, 1024))}

    w2a = w3a = w3f = None
    w2f_wT = w2f_alpha = w2f_bias = None
    if p2:
        w2a = {"W_upT": D("l2a_W_upT", (1024, 1024)), "W_vT": D("l2a_W_vT", (1024, 1024)),
               "W_projT": D("l2a_W_projT", (1024, 1024)), "id_row": D("l2a_id", (1, 1024)),
               "wraw": D("l2a_wraw", (1024, 1024))}
        w2f_wT = D("l2f_wT", (1024, 1024))
        w2f_alpha = D("l2f_alpha", (1, 1))
        w2f_bias = D("l2f_bias", (128, 8))

        w3a = {"wqT": D("l3a_wqT", (1024, 64)), "wkT": D("l3a_wkT", (1024, 64)),
               "wvT": D("l3a_wvT", (1024, 64)), "woT": D("l3a_woT", (64, 1024))}
        w3f = {"routerT": D("l3f_routerT", (1024, 8)), "w1T": D("l3f_w1T", (8, 1024, 4096)),
               "b1": D("l3f_b1", (8, 128, 32)), "w2T": D("l3f_w2T", (8, 4096, 1024)),
               "b2": D("l3f_b2", (8, 1024))}

    TQ = 64 * (T // 128)
    hand = None
    if part in (1, 2):
        kk = "ExternalOutput" if part == 1 else "ExternalInput"
        hand = {
            "qf": Dram(nc.dram_tensor("h_qf", [16, 128, TQ], BF16, kind=kk).ap(), "h_qf"),
            "kdc": Dram(nc.dram_tensor("h_kdc", [16, 128, TQ], F32, kind=kk).ap(), "h_kdc"),
            "v": Dram(nc.dram_tensor("h_v", [1024, T], BF16, kind=kk).ap(), "h_v"),
        }
        if part == 1:
            hand["lmax"] = p.dram_out("h_lmax", [1, 1], F32)
        else:
            for n in ("h_qf", "h_kdc", "h_v"):
                if not hasattr(p, "input_names"):
                    p.input_names = []
                p.input_names.append(n)
            hand["maxes8"] = p.dram_in("h_maxes8", [8, 1], F32)
    if part == 1:
        xs1 = p.dram_out("h_xs1", [1024, T], F32)
        emit_l0_ffn(env, x0, xA, w0f, half=0.5)
        emit_l0_attn(env, xA, xB, w0a)
        emit_l0_ffn(env, xB, xA, w0f, half=0.5)
        emit_l1_ffn(env, xA, xs1, w1f, half=0.5)
        emit_l1_attn(env, xs1, None, w1a, ncores=ncores, phase="A", hand=hand)
        p.barrier()
        return p.finalize(), p

    tern_d = p.dram("l2f_tern", [1024, 1024], BF16)
    acol = p.sbuf("l2f_acol", [128, 1], F32)
    bias_t = p.sbuf("l2f_biast", [128, 8], F32)
    p.dma("sync", lambda e: e.dma_start(out=bias_t[:], in_=w2f_bias.ap),
          reads=[w2f_bias.r], writes=[bias_t.r])

    if part == 2:
        xs1 = p.dram_in("h_xs1", [1024, T], F32)
        emit_l1_attn(env, xs1, xA, w1a, ncores=ncores, phase="B", hand=hand)
    else:
        emit_l0_ffn(env, x0, xA, w0f, half=0.5)
        emit_l0_attn(env, xA, xB, w0a)
        emit_l0_ffn(env, xB, xA, w0f, half=0.5)
        emit_l1_ffn(env, xA, xB, w1f, half=0.5)
        emit_l1_attn(env, xB, xA, w1a, ncores=ncores)
    emit_l1_ffn(env, xA, xB, w1f, half=0.5)

    p.open_scope()
    emit_l2_ternarize(env, w2f_wT, tern_d, w2f_alpha, acol)
    p.close_scope()
    emit_l2_ffn(env, xB, xA, tern_d, acol, bias_t, 0.5)
    emit_l2_attn(env, xA, xB, w2a)
    emit_l2_ffn(env, xB, xA, tern_d, acol, bias_t, 0.5)

    emit_l3_ffn(env, xA, xB, w3f, half=0.5)
    emit_l3_attn(env, xB, xA, w3a)
    emit_l3_ffn(env, xA, xout, w3f, half=0.5)

    p.barrier()
    return p.finalize(), p


def host_prep_weights(inputs, T=4096):
    g = lambda n: np.asarray(inputs[n])
    m = {}
    for a, b in (("Wq", "L0_attn_Wq"), ("Wk", "L0_attn_Wk"), ("Wv", "L0_attn_Wv"),
                 ("Wo", "L0_attn_Wo")):
        m["l0a_" + a] = np.ascontiguousarray(g(b))
    for a, b in (("bq", "L0_attn_bq"), ("bk", "L0_attn_bk"), ("bv", "L0_attn_bv"),
                 ("bo", "L0_attn_bo")):
        m["l0a_" + a] = host_col(g(b))
    cos, sin = rope_tables(T)
    m["l0a_cos"], m["l0a_sin"] = cos, sin
    m["l0f_n1"] = host_col(g("L0_ffn_n1_w"))
    m["l0f_n2"] = host_col(g("L0_ffn_n2_w"))
    m["l0f_qn"] = np.ascontiguousarray(np.tile(g("L0_ffn_qn_w"), 2)[:, None])
    m["l0f_kn"] = np.ascontiguousarray(np.tile(g("L0_ffn_kn_w"), 2)[:, None])
    for a, b in (("wqT", "L0_ffn_wq"), ("wkT", "L0_ffn_wk"), ("wvT", "L0_ffn_wv"),
                 ("woT", "L0_ffn_wo"), ("gateT", "L0_ffn_gate_w"),
                 ("e0_upT", "L0_ffn_e0_up"), ("e0_downT", "L0_ffn_e0_down"),
                 ("e1_w1T", "L0_ffn_e1_w1"), ("e1_w2T", "L0_ffn_e1_w2"),
                 ("e1_downT", "L0_ffn_e1_down"), ("e2_l1T", "L0_ffn_e2_l1"),
                 ("e2_l2T", "L0_ffn_e2_l2"), ("e2_l3T", "L0_ffn_e2_l3"),
                 ("e2_l4T", "L0_ffn_e2_l4"), ("e3_downT", "L0_ffn_e3_down"),
                 ("e3_upT", "L0_ffn_e3_up"), ("e3_outT", "L0_ffn_e3_out")):
        m["l0f_" + a] = host_T(g(b))
    m["l1a_Wq"] = np.ascontiguousarray(g("L1_attn_Wq").reshape(1024, 1024))
    m["l1a_Wk"] = np.ascontiguousarray(g("L1_attn_Wk").reshape(1024, 1024))
    m["l1a_Wv"] = np.ascontiguousarray(g("L1_attn_Wv").reshape(1024, 1024))
    m["l1a_Wo"] = np.ascontiguousarray(g("L1_attn_Wo").reshape(1024, 1024))
    m["l1a_projT"] = host_T(g("L1_attn_proj"))
    m["l1a_W1"] = np.ascontiguousarray(g("L1_attn_W1"))
    m["l1a_W2"] = np.ascontiguousarray(g("L1_attn_W2"))
    m["l1a_b1"] = np.ascontiguousarray(g("L1_attn_b1").reshape(32, 128).T)
    m["l1a_b2"] = host_col(g("L1_attn_b2"))
    m["l1a_ln1g"] = host_col(g("L1_attn_ln1_g"))
    m["l1a_ln1b"] = host_col(g("L1_attn_ln1_b"))
    m["l1a_ln2g"] = host_col(g("L1_attn_ln2_g"))
    m["l1a_ln2b"] = host_col(g("L1_attn_ln2_b"))
    m["l1f_routerT"] = host_T(g("L1_ffn_router_w"))
    m["l1f_ebias"] = np.ascontiguousarray(g("L1_ffn_expert_bias")[:, None])
    m["l1f_w1T"] = np.ascontiguousarray(g("L1_ffn_w1").transpose(0, 2, 1))
    m["l1f_w3T"] = np.ascontiguousarray(g("L1_ffn_w3").transpose(0, 2, 1))
    m["l1f_w2T"] = np.ascontiguousarray(g("L1_ffn_w2").transpose(0, 2, 1))
    m["l1f_sw1T"] = np.ascontiguousarray(g("L1_ffn_sw1").transpose(0, 2, 1))
    m["l1f_sw3T"] = np.ascontiguousarray(g("L1_ffn_sw3").transpose(0, 2, 1))
    m["l1f_sw2T"] = np.ascontiguousarray(g("L1_ffn_sw2").transpose(0, 2, 1))
    m["l2a_W_upT"] = host_T(g("L2_attn_W_up"))
    m["l2a_W_vT"] = host_T(g("L2_attn_W_v"))
    m["l2a_W_projT"] = host_T(g("L2_attn_W_proj"))
    m["l2a_id"] = np.ascontiguousarray(g("L2_attn_identity")[None, :])
    m["l2a_wraw"] = np.ascontiguousarray(g("L2_attn_W_scan_raw"))
    m["l2f_wT"] = host_T(g("L2_ffn_weight"))
    m["l2f_alpha"] = np.ascontiguousarray(g("L2_ffn_alpha").reshape(1, 1))
    m["l2f_bias"] = host_col(g("L2_ffn_bias"))
    m["l3a_wqT"] = host_T(g("L3_attn_wq"))
    m["l3a_wkT"] = host_T(g("L3_attn_wk"))
    m["l3a_wvT"] = host_T(g("L3_attn_wv"))
    m["l3a_woT"] = host_T(g("L3_attn_wo"))
    m["l3f_routerT"] = host_T(g("L3_ffn_router_w"))
    m["l3f_w1T"] = np.ascontiguousarray(g("L3_ffn_w1").transpose(0, 2, 1))
    m["l3f_b1"] = np.ascontiguousarray(
        g("L3_ffn_b1").reshape(8, 32, 128).transpose(0, 2, 1))
    m["l3f_w2T"] = np.ascontiguousarray(g("L3_ffn_w2").transpose(0, 2, 1))
    m["l3f_b2"] = np.ascontiguousarray(g("L3_ffn_b2"))
    return m


FUSED = False


def kernel(**inputs):
    T = 4096
    x = np.asarray(inputs["L0_attn_x"])
    B = x.shape[0]
    wmap = host_prep_weights(inputs, T)
    if FUSED:
        nc, p = build_module(T=T, ncores=B)
        wmap.update(p.const_arrays)
        in_maps = []
        for b in range(B):
            im = dict(wmap)
            im["x0"] = np.ascontiguousarray(x[b].T)
            in_maps.append(im)
        res = run_bass_kernel_spmd(nc, in_maps, core_ids=list(range(B)))
        return np.stack([np.asarray(r["xout"]).T for r in res.results],
                        axis=0).astype(np.float32)
    nc1, p1 = build_module(T=T, ncores=B, part=1)
    full1 = dict(wmap)
    full1.update(p1.const_arrays)
    in_maps = []
    for b in range(B):
        im = {k: full1[k] for k in p1.input_names if k != "x0"}
        im["x0"] = np.ascontiguousarray(x[b].T)
        in_maps.append(im)
    r1 = run_bass_kernel_spmd(nc1, in_maps, core_ids=list(range(B))).results
    maxes8 = np.ascontiguousarray(
        np.concatenate([np.asarray(r["h_lmax"]) for r in r1], axis=0))
    nc2, p2 = build_module(T=T, ncores=B, part=2)
    full2 = dict(wmap)
    full2.update(p2.const_arrays)
    hand_keys = ("h_xs1", "h_qf", "h_kdc", "h_v", "h_maxes8")
    in_maps2 = []
    for b in range(B):
        im = {k: full2[k] for k in p2.input_names if k not in hand_keys}
        for k in ("h_xs1", "h_qf", "h_kdc", "h_v"):
            im[k] = np.asarray(r1[b][k])
        im["h_maxes8"] = maxes8
        in_maps2.append(im)
    res = run_bass_kernel_spmd(nc2, in_maps2, core_ids=list(range(B)))
    return np.stack([np.asarray(r["xout"]).T for r in res.results],
                    axis=0).astype(np.float32)
